# version 41
# baseline (speedup 1.0000x reference)
import numpy as np
import ml_dtypes
import concourse.bass as bass
import concourse.mybir as mybir
from concourse.bass_utils import run_bass_kernel_spmd

F32 = mybir.dt.float32
BF16 = mybir.dt.bfloat16
ACT = mybir.ActivationFunctionType
OP = mybir.AluOpType
AXL = mybir.AxisListType

BF = ml_dtypes.bfloat16

D = 1024
ND = 8
NI = 4
INNER = 512
H = 8
HD = 64
CK = 128
EPS = 1e-5

TCH = 3
MAXW = TCH * CK
TILES = (TCH,) * 11
NCH = sum(TILES)
ROWS = NCH * CK
OUT_ROWS = (NCH - 1) * CK

ENGINES = ("tensor", "vector", "scalar", "sync", "gpsimd")


def host_constants():
    ident = np.eye(CK, dtype=np.float32).astype(BF)
    tri = np.triu(np.ones((CK, CK), np.float32))
    msu = np.tril(np.ones((CK, CK), np.float32), -1)

    def row_lvl(anchor):
        C = np.zeros((CK, CK), np.float32)
        for t in range(CK):
            B = anchor(t)
            if B is None:
                continue
            if t >= B:
                C[B:t + 1, t] = 1.0
            else:
                C[t + 1:B, t] = -1.0
        return C

    def col_lvl(anchor):
        C = np.zeros((CK, CK), np.float32)
        for s in range(CK):
            B = anchor(s)
            if B is None:
                continue
            if s < B:
                C[s + 1:B, s] = 1.0
            else:
                C[B:s + 1, s] = -1.0
        return C

    r64 = row_lvl(lambda t: 64 if t >= 64 else None)
    r32 = row_lvl(lambda t: 32 if 32 <= t < 64 else (96 if t >= 96 else None))
    r16 = row_lvl(lambda t: {1: 16, 3: 48, 5: 80, 7: 112}.get(t // 16))
    rdg = row_lvl(lambda t: 16 * (t // 16) + 8)
    c64 = col_lvl(lambda s: 64 if s < 64 else None)
    c32 = col_lvl(lambda s: 32 if s < 32 else (96 if 64 <= s < 96 else None))
    c16 = col_lvl(lambda s: {0: 16, 2: 48, 4: 80, 6: 112}.get(s // 16))
    cdg = col_lvl(lambda s: 16 * (s // 16) + 8)

    p1a = np.concatenate([tri, r64 + c64, r32 + c32,
                          np.ones((CK, 1), np.float32)], 1).astype(BF)
    p2n = np.concatenate([r16 + c16, rdg, cdg], 1).astype(BF)
    mask4 = np.tile(np.triu(np.ones((CK, CK), np.uint8)), (1, 4))
    onesc = np.ones((CK, 1), np.float32).astype(BF)
    fmisc = np.tile(np.array([[EPS, 1.0, -1.0, 0.5, -0.5, 0.0, 2.0, 0.25]],
                             np.float32), (CK, 1))
    return dict(cident=ident, cp1a=p1a, cp2n=p2n,
                cmsu=msu.astype(BF), cmask=mask4, conesc=onesc, cfmisc=fmisc)


class Prog:
    def __init__(self, nc, stack):
        self.nc = nc
        self.stack = stack
        self.streams = {e: [] for e in ENGINES}
        self.sems = {}
        self.ctr = {}
        self.marks = {}
        self.pending = {}
        self.autosem = {}

    def sem(self, name):
        h = self.stack.enter_context(self.nc.semaphore(name))
        self.sems[name] = h
        self.ctr[name] = 0
        return name

    def emit(self, eng, fn, waits=(), inc=None, inc_n=1, consumes=(), vhz=False):
        ws = []
        pend = self.pending.pop(eng, None)
        if pend:
            waits = list(waits) + pend
        auto = self.autosem.get(eng)
        if auto is not None:
            if self.ctr[auto] > 0:
                waits = list(waits) + [(auto, self.ctr[auto])]
            if inc is None:
                inc = auto
                inc_n = 1
        for wv in waits:
            if wv is None:
                continue
            s, v = wv
            assert v <= self.ctr[s], (s, v, self.ctr[s])
            if v > 0:
                ws.append((s, v))
        wd = {}
        for s, v in ws:
            wd[s] = max(wd.get(s, 0), v)
        ws = list(wd.items())
        if inc is not None:
            self.ctr[inc] += inc_n
        self.streams[eng].append((fn, tuple(ws), (inc, inc_n) if inc else None))
        for bi in consumes:
            assert inc is not None
            self.banks.note(bi, inc)

    def barrier(self, engines, waits):
        for e in engines:
            self.pending.setdefault(e, []).extend(waits)

    def mark(self, name, sem):
        self.marks[name] = (sem, self.ctr[sem])

    def w(self, name):
        return self.marks[name]

    def wo_(self, name):
        return self.marks.get(name)

    def finalize(self):
        nc = self.nc
        own_sem = {"tensor": ("sT", 0), "vector": ("sV", 1),
                   "scalar": ("sS", 1), "gpsimd": ("sG", 1)}
        with nc.Block() as block:
            def run_stream(eng_name):
                own, strict = own_sem.get(eng_name, (None, 0))
                can_attach = eng_name in ("tensor", "vector", "scalar")
                def body(e):
                    wm = {}
                    own_cnt = 0
                    for fn, ws, inc in self.streams[eng_name]:
                        attach = None
                        standalone = []
                        for s, v in ws:
                            if s == own and v + strict <= own_cnt:
                                continue
                            if wm.get(s, 0) >= v:
                                continue
                            standalone.append((s, v))
                            wm[s] = v
                        if attach is None and standalone and can_attach:
                            attach = standalone.pop()
                        for s, v in standalone:
                            e.wait_ge(self.sems[s], v)
                        inst = fn(e)
                        if attach is not None:
                            inst._wait_ge(self.sems[attach[0]], attach[1])
                        if inc is not None:
                            inst.then_inc(self.sems[inc[0]], inc[1])
                            if inc[0] == own:
                                own_cnt += inc[1]
                return body
            block.tensor(run_stream("tensor"))
            block.vector(run_stream("vector"))
            block.scalar(run_stream("scalar"))
            block.sync(run_stream("sync"))
            block.gpsimd(run_stream("gpsimd"))


class Banks:
    def __init__(self, p, tensors):
        self.p = p
        self.t = tensors
        self.waits = [[] for _ in tensors]
        self.rr = 0

    def acquire(self):
        i = self.rr % len(self.t)
        self.rr += 1
        w = self.waits[i]
        self.waits[i] = []
        return i, list(w)

    def note(self, i, sem):
        self.waits[i].append((sem, self.p.ctr[sem]))


def build_program():
    from contextlib import ExitStack
    nc = bass.Bass("TRN2", target_bir_lowering=False)
    stack = ExitStack()

    def din(name, shape, dt=F32):
        return nc.dram_tensor(name, list(shape), dt, kind="ExternalInput")

    xm = din("xm", (ROWS, D))
    xf = din("xf", (ROWS, D))
    out = nc.dram_tensor("out", [OUT_ROWS, D], F32, kind="ExternalOutput")

    wsel = din("wsel", (D, 160))
    dtw = din("dtw", (32, INNER))
    wk = din("wk", (D, INNER))
    wv = din("wv", (D, INNER))
    wr = din("wr", (D, INNER))
    wg = din("wg", (D, INNER))
    wo = din("wo", (INNER, D))
    fwr = din("fwr", (D, D))
    fwk = din("fwk", (D, D))
    fwv = din("fwv", (D, D))
    dtb_b = din("dtb_b", (CK, INNER))
    base_b = din("base_b", (CK, INNER))
    tf_b = din("tf_b", (CK, INNER))
    mur = din("mur", (CK, ND))
    muk = din("muk", (CK, ND))
    cident = din("cident", (CK, CK), BF16)
    cp1a = din("cp1a", (CK, 385), BF16)
    cp2n = din("cp2n", (CK, 384), BF16)
    cmsu = din("cmsu", (CK, CK), BF16)
    cmask = din("cmask", (CK, 4 * CK), mybir.dt.uint8)
    conesc = din("conesc", (CK, 1), BF16)
    cfmisc = din("cfmisc", (CK, 8), F32)

    def s_(name, shape, dt):
        return stack.enter_context(nc.sbuf_tensor(name, list(shape), dt))

    wk_sb = s_("wk_sb", (CK, ND * INNER), BF16)
    wv_sb = s_("wv_sb", (CK, ND * INNER), BF16)
    wr_sb = s_("wr_sb", (CK, ND * INNER), BF16)
    wg_sb = s_("wg_sb", (CK, ND * INNER), BF16)
    wo_sb = s_("wo_sb", (CK, NI * D), BF16)
    fwrT = s_("fwrT", (CK, 64 * CK), BF16)
    fwkT = s_("fwkT", (CK, 64 * CK), BF16)
    fwvT = s_("fwvT", (CK, 64 * CK), BF16)
    wsm_sb = s_("wsm_sb", (CK, ND * 32), BF16)
    wbar_sb = s_("wbar_sb", (CK, ND), BF16)
    dtw_sb = s_("dtw_sb", (32, INNER), BF16)

    ident_sb = s_("ident_sb", (CK, CK), BF16)
    p1a_sb = s_("p1a_sb", (CK, 385), BF16)
    p2n_sb = s_("p2n_sb", (CK, 384), BF16)
    msu_sb = s_("msu_sb", (CK, CK), BF16)
    mask_sb = s_("mask_sb", (CK, 4 * CK), mybir.dt.uint8)
    ones_sb = s_("ones_sb", (CK, 1), BF16)
    dtbB = s_("dtbB", (CK, INNER), BF16)
    baseB = s_("baseB", (CK, INNER), BF16)
    tfB = s_("tfB", (CK, INNER), BF16)
    mur_sb = s_("mur_sb", (CK, ND), F32)
    muk_sb = s_("muk_sb", (CK, ND), F32)
    fmisc_sb = s_("fmisc_sb", (CK, 8), F32)


    xch = s_("xch", (CK, 5 * D), F32)
    xln2 = s_("xln2", (CK, D), BF16)

    stats = s_("stats", (CK, 64), F32)
    lncols = s_("lncols", (CK, 8), F32)
    xln = s_("xln", (CK, D), BF16)
    ln1fm = s_("ln1fm", (CK, ND * MAXW), BF16)
    kfm = s_("kfm", (CK, NI * MAXW), BF16)
    draw_sb = s_("draw_sb", (32, MAXW), BF16)
    spy = s_("spy", (CK, INNER), BF16)
    ldt = s_("ldt", (CK, INNER), BF16)
    ev = s_("ev", (CK, 2 * 4 * 768), BF16)
    dcols = s_("dcols", (CK, 8), F32)
    eend = s_("eend", (CK, INNER), BF16)
    ktokr = s_("ktokr", (CK, 2 * INNER), BF16)
    kct = s_("kct", (CK, 2 * INNER), BF16)
    vraw = s_("vraw", (CK, INNER), BF16)
    kvprod = s_("kvprod", (CK, 2 * INNER), BF16)
    vbt = s_("vbt", (CK, 4 * INNER), BF16)
    gp_sb = s_("gp_sb", (CK, 6 * INNER), BF16)
    a_sb = s_("a_sb", (CK, H * CK), BF16)
    S_sb = s_("S_sb", (CK, 4 * HD), F32)
    Sbf = s_("Sbf", (CK, 4 * HD), BF16)
    ys_sb = s_("ys_sb", (CK, 2 * INNER), BF16)
    bny = s_("bny", (CK, INNER), BF16)
    ysfm = s_("ysfm", (CK, NI * CK), BF16)
    bcol = s_("bcol", (CK, 2), F32)
    x2 = s_("x2", (CK, TCH * D), F32)
    x2fm = s_("x2fm", (CK, ND * (MAXW + 1)), BF16)
    xr_sb = s_("xr_sb", (CK, ND * MAXW), BF16)
    mixd = s_("mixd", (CK, MAXW), BF16)
    xk_sb = s_("xk_sb", (CK, ND * MAXW), BF16)
    kact = s_("kact", (CK, ND * MAXW), BF16)

    rgtmp = xln
    stg = x2[:, 0:INNER]
    stg2 = x2[:, INNER:2 * INNER]
    wnat = xr_sb[:, 0:INNER]

    P = Prog(nc, stack)
    pb = [stack.enter_context(nc.psum_tensor(f"pb{i}", [CK, INNER], F32))
          for i in range(8)]
    banks = Banks(P, pb)
    P.banks = banks

    sT = P.sem("sT")
    sV = P.sem("sV")
    sS = P.sem("sS")
    sG = P.sem("sG")
    sDX = P.sem("sDX")
    sDW = P.sem("sDW")
    sDW1 = P.sem("sDW1")
    sDC = P.sem("sDC")
    sDO = P.sem("sDO")

    T, V, Ss, SY, G = "tensor", "vector", "scalar", "sync", "gpsimd"
    P.autosem = {V: sV, Ss: sS, G: sG}

    def mm(psl, lhs, rhs, start, stop, tp=None, waits=(), inc=None):
        P.emit(T, (lambda e, psl=psl, lhs=lhs, rhs=rhs, st=start, sp=stop, tp=tp:
                   e.matmul(psl, lhs, rhs, start=st, stop=sp, tile_position=tp)),
               waits=waits, inc=inc)

    nw = {sDW: 0, sDW1: 0}

    def dma_w(dst_ap, src_ap, waits=(), sem=None):
        sem = sDW if sem is None else sem
        nw[sem] += 1
        P.emit(G, (lambda e, d=dst_ap, s=src_ap: e.dma_start(d, s)),
               waits=waits, inc=sem, inc_n=16)
        return (sem, 16 * nw[sem])

    ncst = 0
    for dst, src in ((ident_sb, cident), (p1a_sb, cp1a), (p2n_sb, cp2n),
                     (msu_sb, cmsu),
                     (mask_sb, cmask), (ones_sb, conesc), (mur_sb, mur),
                     (muk_sb, muk), (fmisc_sb, cfmisc)):
        ncst += 1
        P.emit(SY, (lambda e, d=dst[:, :], s=src[:, :]: e.dma_start(d, s)),
               inc=sDC, inc_n=16)
    cdone = 16 * ncst
    for col, val in enumerate((EPS, 1.0, -1.0, 0.5, -0.5, 0.0, 2.0, 0.25)):
        nc.const_aps.register(F32, val, fmisc_sb[:, col:col + 1])
    P.emit(V, lambda e: e.memset(a_sb[:, :], 0.0))
    P.emit(V, lambda e: e.memset(S_sb[:, :], 0.0))
    P.emit(V, lambda e: e.memset(Sbf[:, :], 0.0))
    P.emit(V, lambda e: e.memset(x2fm[:, :], 0.0), inc=sV)
    P.mark("setup_v", sV)

    stgs = [(stg, sDW), (stg2, sDW1)]
    stg_i = [0]

    def stg_next():
        i = stg_i[0] % 2
        stg_i[0] += 1
        t, sem = stgs[i]
        return t, f"stg{i}_free", sem

    def load_plan():
        for cd, dst in ((dtb_b, dtbB), (base_b, baseB), (tf_b, tfB)):
            sg, sfree, ssem = stg_next()
            mark = dma_w(sg[:, :], cd[:, :], waits=[P.wo_(sfree)], sem=ssem)
            P.emit(V, (lambda e, d=dst, sg=sg: e.tensor_copy(d[:, :], sg[:, :])),
                   waits=[mark], inc=sV)
            P.mark(sfree, sV)
            yield
        P.mark("w_bc", sV)
        src = wsel[:, :].rearrange("(dc p) o -> p dc o", p=CK)
        for q4 in range(4):
            sg, sfree, ssem = stg_next()
            mark = dma_w(sg[:, 0:2 * 160], src[:, 2 * q4:2 * q4 + 2, :],
                         waits=[P.wo_(sfree)], sem=ssem)
            P.emit(V, (lambda e, q4=q4, sg=sg: e.tensor_copy(
                wsm_sb[:, :].rearrange("p (dc o) -> p dc o", dc=ND)[
                    :, 2 * q4:2 * q4 + 2, :],
                sg[:, 0:2 * 160].rearrange("p (dc o) -> p dc o", dc=2)[:, :, 0:32])),
                waits=[mark], inc=sV)
            for q in range(2):
                dc = 2 * q4 + q
                P.emit(V, (lambda e, q=q, sg=sg: e.reduce_sum(
                    lncols[:, 6:7], sg[:, q * 160 + 32:q * 160 + 96], axis=AXL.X)),
                    inc=sV, vhz=True)
                P.emit(V, (lambda e, dc=dc: e.tensor_scalar(
                    wbar_sb[:, dc:dc + 1], lncols[:, 6:7], 1.0 / 64, None, OP.mult)),
                    inc=sV, vhz=True)
            P.mark(sfree, sV)
            yield
        sg, sfree, ssem = stg_next()
        mark = dma_w(sg[0:32, 0:INNER], dtw[:, :], waits=[P.wo_(sfree)], sem=ssem)
        P.emit(V, (lambda e, sg=sg: e.tensor_copy(dtw_sb[:, :], sg[0:32, 0:INNER])),
               waits=[mark], inc=sV)
        P.mark(sfree, sV)
        P.mark("w_dt", sV)
        yield
        for wdram, wsb, name in ((wk, wk_sb, "w_k"), (wv, wv_sb, "w_v"),
                                 (wr, wr_sb, "w_r"), (wg, wg_sb, "w_g")):
            for dc in range(ND):
                sg, sfree, ssem = stg_next()
                mark = dma_w(sg[:, :], wdram[dc * CK:(dc + 1) * CK, :],
                             waits=[P.wo_(sfree)], sem=ssem)
                dst = wsb[:, dc * INNER:(dc + 1) * INNER]
                P.emit(V, (lambda e, d=dst, sg=sg: e.tensor_copy(d, sg[:, :])),
                       waits=[mark], inc=sV)
                P.mark(sfree, sV)
                yield
            P.mark(name, sV)
        for ic in range(NI):
            for hf in range(2):
                sg, sfree, ssem = stg_next()
                mark = dma_w(sg[:, :],
                             wo[ic * CK:(ic + 1) * CK, hf * INNER:(hf + 1) * INNER],
                             waits=[P.wo_(sfree)], sem=ssem)
                dst = wo_sb[:, ic * D + hf * INNER:ic * D + (hf + 1) * INNER]
                P.emit(V, (lambda e, d=dst, sg=sg: e.tensor_copy(d, sg[:, :])),
                    waits=[mark], inc=sV)
                P.mark(sfree, sV)
                yield
        P.mark("w_o", sV)
        for wdram, wsbT, scale, name in ((fwr, fwrT, None, "w_fr"),
                                         (fwk, fwkT, None, "w_fk"),
                                         (fwv, fwvT, None, "w_fv")):
            for rb in range(ND):
                for ch in range(2):
                    sg, sfree, ssem = stg_next()
                    mark = dma_w(sg[:, :],
                                 wdram[rb * CK:(rb + 1) * CK,
                                       ch * INNER:(ch + 1) * INNER],
                                 waits=[P.wo_(sfree)], sem=ssem)
                    if scale is None:
                        P.emit(V, (lambda e, sg=sg: e.tensor_copy(wnat[:, :], sg[:, :])),
                               waits=[mark, P.wo_("wnat_free")], inc=sV)
                    else:
                        P.emit(V, (lambda e, sc=scale, sg=sg: e.tensor_scalar(
                            wnat[:, :], sg[:, :], sc, None, OP.mult)),
                            waits=[mark, P.wo_("wnat_free")], inc=sV)
                    P.mark(sfree, sV)
                    P.mark("wnat", sV)
                    bi, bw = banks.acquire()
                    for q in range(4):
                        w0 = (list(bw) + ([P.w("wnat"), (sDC, cdone)]
                                          if q == 0 else []))
                        mm(pb[bi][:, q * CK:(q + 1) * CK],
                           wnat[:, q * CK:(q + 1) * CK], ident_sb[:, :],
                           True, True, waits=w0,
                           inc=(sT if q == 3 else None))
                    P.mark("wtr", sT)
                    P.mark("wnat_free", sT)
                    dst = wsbT[:, :].rearrange(
                        "p (cc rb j) -> p cc rb j", cc=ND, rb=ND)[
                        :, ch * 4:(ch + 1) * 4, rb, :]
                    P.emit(Ss, (lambda e, d=dst, s2=pb[bi][:, 0:4 * CK]:
                                e.copy(d, s2)),
                           waits=[P.w("wtr")], consumes=[bi])
                    yield
            P.mark(name, sS)

    loader = load_plan()
    loads_left = [True]

    def pump_loader(n=1):
        for _ in range(n):
            if not loads_left[0]:
                return
            try:
                next(loader)
            except StopIteration:
                loads_left[0] = False

    pump_loader(3 + 4 + 1 + 8 + 8)

    ndx = [0]
    xmark = {}

    def dma_x(xsrc, key, gchunk, slot, waits=()):
        ndx[0] += 1
        src = xsrc[gchunk * CK:(gchunk + 1) * CK, :]
        P.emit(SY, (lambda e, s=src, sl=slot: e.dma_start(
            xch[:, sl * D:(sl + 1) * D], s)), waits=waits, inc=sDX, inc_n=16)
        xmark[(key, gchunk)] = 16 * ndx[0]

    def xwait(key, gchunk):
        return (sDX, xmark[(key, gchunk)])

    def ln_stats(xap, rcol, ncol, xwait, so=0):
        s0 = 16 * so
        rsm = f"rstd{so}"
        P.emit(V, (lambda e, a=xap, s0=s0: e.bn_stats(
            stats[:, s0:s0 + 6], a[:, 0:INNER])),
               waits=[xwait, P.wo_(rsm)], inc=sV)
        P.emit(V, (lambda e, a=xap, s0=s0: e.bn_stats(
            stats[:, s0 + 6:s0 + 12], a[:, INNER:D])),
               inc=sV)
        P.emit(V, (lambda e, s0=s0: e.bn_aggr(stats[:, s0 + 12:s0 + 14],
                                              stats[:, s0:s0 + 12])), inc=sV)
        P.mark("aggr", sV)
        P.emit(Ss, (lambda e, s0=s0: e.activation(
            stats[:, s0 + 14:s0 + 15], stats[:, s0 + 13:s0 + 14],
            ACT.Ln, bias=EPS)),
               waits=[P.w("aggr")], inc=sS)
        P.emit(Ss, (lambda e, rc=rcol, s0=s0: e.activation(
            lncols[:, rc:rc + 1], stats[:, s0 + 14:s0 + 15],
            ACT.Exp, scale=-0.5)), inc=sS)
        P.mark(rsm, sS)
        P.emit(V, (lambda e, rc=rcol, ncc=ncol, s0=s0: e.tensor_scalar(
            lncols[:, ncc:ncc + 1], stats[:, s0 + 12:s0 + 13],
            lncols[:, rc:rc + 1], -1.0,
            OP.mult, OP.mult)),
            waits=[P.w(rsm), P.wo_("xln1n"), P.wo_("xln2n")], inc=sV)
        P.mark("lncols", sV)
        return now(sV)

    def transpose_group(src_fn, n, dst_fn, waits, extra_evac_waits=()):
        first = True
        for g0 in range(0, n, 4):
            gn = min(4, n - g0)
            bi, bw = banks.acquire()
            for j in range(gn):
                w0 = list(bw) + (list(waits) if (first and j == 0) else [])
                first = False
                mm(pb[bi][:, j * CK:(j + 1) * CK], src_fn(g0 + j),
                   ident_sb[:, :], True, True, waits=w0,
                   inc=(sT if j == gn - 1 else None))
            P.mark("trg", sT)
            dst = dst_fn(g0, gn)
            src2 = pb[bi][:, 0:gn * CK]
            P.emit(Ss, (lambda e, d=dst, s=src2: e.copy(d, s)),
                   waits=[P.w("trg")] + list(extra_evac_waits), consumes=[bi])

    LNP = ((0, 1), (2, 3), (6, 7))

    def ln1_chunk(key, gc, slot, ci):
        assert (key, gc) in xmark
        rcol, ncol = LNP[ci % 3]
        lnc = ln_stats(xch[:, slot * D:(slot + 1) * D], rcol, ncol,
                       xwait(key, gc), so=ci % 3)
        P.emit(Ss, (lambda e, sl=slot, rc=rcol, ncc=ncol: e.activation(
            xln[:, :], xch[:, sl * D:(sl + 1) * D],
            ACT.Identity, bias=lncols[:, ncc:ncc + 1],
            scale=lncols[:, rc:rc + 1])),
            waits=[lnc, P.wo_("ln1tr")], inc=sS)
        P.mark("xln1n", sS)

        def srcT(j):
            return xln[:, j * CK:(j + 1) * CK]

        def dstT(g0, gn, ci=ci):
            r = ln1fm[:, :].rearrange("p (dc w) -> p dc w", dc=ND)
            return r[:, g0:g0 + gn, ci * CK:(ci + 1) * CK]

        transpose_group(srcT, ND, dstT,
                        [P.w("xln1n"), (sDC, cdone)],
                        extra_evac_waits=[P.wo_("ln1fm_free")])
        P.mark("ln1tr", sT)
        P.mark("ln1fm_c", sS)

    def dt_path_a(lhs_draw):
        bE, wE = banks.acquire()
        mm(pb[bE][:, :], lhs_draw, dtw_sb[:, :], True, True,
           waits=list(wE) + [P.w("drawc"), P.wo_("w_dt")], inc=sT)
        P.mark("dt2", sT)
        P.emit(V, (lambda e, b=bE: e.tensor_add(spy[:, :], pb[b][:, :], dtbB[:, :])),
               waits=[P.w("dt2"), P.wo_("sp_s"), P.wo_("w_bc"),
                      P.wo_("spy_free2")], consumes=[bE])
        P.mark("spz", sV)
        P.emit(Ss, (lambda e: e.activation(spy[:, :], spy[:, :], ACT.Exp)),
               waits=[P.w("spz")], inc=sS)
        P.emit(Ss, (lambda e: e.activation(spy[:, :], spy[:, :], ACT.Ln, bias=1.0)),
               inc=sS)
        P.mark("sp_s", sS)

    def dt_path_b(gc=0):
        lds = ldt[:, :] if gc % 2 == 0 else vraw[:, 0:INNER]
        P.emit(V, (lambda e, lds=lds: e.tensor_mul(lds, spy[:, :], baseB[:, :])),
               waits=[P.w("sp_s"), P.wo_(f"ld_free{gc % 2}")], inc=sV)
        P.mark("ld", sV)

    def eend_path(ldc, sl2=0):
        eb = eend[:, :] if sl2 == 0 else kvprod[:, 0:INNER]
        bE2, wE2 = banks.acquire()
        mm(pb[bE2][:, :], msu_sb[:, :], ldc, True, True,
           waits=list(wE2) + [P.w("ld")], inc=sT)
        P.mark("eendm", sT)
        P.emit(Ss, (lambda e, b=bE2, eb=eb: e.activation(
            eb, pb[b][:, :], ACT.Exp, scale=-1.0)),
            waits=[P.w("eendm"), P.wo_(f"eend_free{sl2}")], consumes=[bE2])
        P.mark("eend", sS)
        return eb

    def dcol_rep(c0):
        b = dcols[:, c0:c0 + 4]
        return bass.AP(b.tensor, b.offset, [b.ap[0], b.ap[-1], [0, HD]])

    def state_chain(bU, extra=()):
        P.emit(V, (lambda e: e.tensor_mul(S_sb[:, :], S_sb[:, :], dcol_rep(0))),
               waits=[P.w("um"), P.w("dcol")] + list(extra))
        P.emit(V, (lambda e, b=bU: e.tensor_add(S_sb[:, :], S_sb[:, :],
                                                pb[b][:, 0:4 * HD])),
               consumes=[bU])
        P.emit(V, lambda e: e.tensor_copy(Sbf[:, :], S_sb[:, :]), inc=sV)
        P.mark("chain", sV)

    def fast_ln1a(gc):
        if gc + 2 < NCH:
            dma_x(xf, "xf", gc + 2, (gc + 2) % 3, waits=[P.wo_("xln1n")])
        slot = gc % 3
        ln_stats(xch[:, slot * D:(slot + 1) * D], 0, 1, xwait("xf", gc), so=0)

    def fast_ln1b(gc):
        slot = gc % 3
        ci = gc % 2
        P.emit(Ss, (lambda e, sl=slot: e.activation(
            xln[:, :], xch[:, sl * D:(sl + 1) * D],
            ACT.Identity, bias=lncols[:, 1:2], scale=lncols[:, 0:1])),
            waits=[P.w("lncols"), P.wo_("ln1tr")], inc=sS)
        P.mark("xln1n", sS)

        def srcT(j):
            return xln[:, j * CK:(j + 1) * CK]

        def dstT(g0, gn, ci=ci):
            r = ln1fm[:, :].rearrange("p (dc w) -> p dc w", dc=ND)
            return r[:, g0:g0 + gn, ci * CK:(ci + 1) * CK]

        transpose_group(srcT, ND, dstT,
                        [P.w("xln1n"), (sDC, cdone)],
                        extra_evac_waits=[P.wo_("ln1fm_free")])
        P.mark("ln1tr", sT)
        P.mark("ln1fm_c", sS)

    def fast_projMM(gc):
        sl2 = gc % 2
        lfm = [P.w("ln1fm_c")]
        bK, wK = banks.acquire()
        bV2, wV2 = banks.acquire()
        bB, wB = banks.acquire()
        for dc in range(ND):
            st, sp = dc == 0, dc == ND - 1
            lhs = ln1fm[:, dc * MAXW + sl2 * CK:dc * MAXW + (sl2 + 1) * CK]
            w0 = (list(wK) + list(wV2) + list(wB) + lfm +
                  [P.wo_("w_k"), P.wo_("w_v")]) if dc == 0 else []
            mm(pb[bK][:, :], lhs, wk_sb[:, dc * INNER:(dc + 1) * INNER], st, sp,
               waits=w0)
            mm(pb[bV2][:, :], lhs, wv_sb[:, dc * INNER:(dc + 1) * INNER], st, sp)
            mm(pb[bB][:, 0:1], lhs, wbar_sb[:, dc:dc + 1], st, sp,
               inc=(sT if sp else None))
        P.mark("rsf", sT)
        return bK, bV2, bB

    def fast_projrest(gc, bK, bV2, bB):
        sl2 = gc % 2
        bD, wD = banks.acquire()
        for dc in range(ND):
            mm(pb[bD][0:32, 0:CK], wsm_sb[:, dc * 32:(dc + 1) * 32],
               ln1fm[:, dc * MAXW + sl2 * CK:dc * MAXW + (sl2 + 1) * CK],
               dc == 0, dc == ND - 1,
               waits=(list(wD) if dc == 0 else []),
               inc=(sT if dc == ND - 1 else None))
        P.mark("drawm", sT)
        P.mark("ln1fm_free", sT)
        P.emit(Ss, (lambda e, b=bD, sl2=sl2: e.copy(
            draw_sb[:, sl2 * CK:(sl2 + 1) * CK], pb[b][0:32, 0:CK])),
            waits=[P.w("drawm"), P.wo_("draw_free")], consumes=[bD])
        P.mark("drawc", sS)
        P.emit(V, (lambda e, b=bK, sl2=sl2: e.tensor_copy(
            ktokr[:, sl2 * INNER:(sl2 + 1) * INNER], pb[b][:, :])),
            waits=[P.w("rsf"), P.wo_("ktok")], consumes=[bK])
        P.emit(V, (lambda e, b=bB: e.tensor_copy(bcol[:, 0:1], pb[b][:, 0:1])),
               consumes=[bB])
        P.emit(V, (lambda e, b=bV2, sl2=sl2: e.tensor_scalar(
            vbt[:, sl2 * INNER:(sl2 + 1) * INNER], pb[b][:, :], bcol[:, 0:1],
            None, OP.mult)), waits=[P.wo_("um")], consumes=[bV2], vhz=True)
        P.mark("kvb", sV)

    def fast_dt_a(gc):
        sl2 = gc % 2
        dt_path_a(draw_sb[:, sl2 * CK:(sl2 + 1) * CK])
        P.mark("draw_free", sT)

    def fast_b1(gc):
        ldc = ldt[:, :] if gc % 2 == 0 else vraw[:, 0:INNER]
        fast_eb[0] = eend_path(ldc, gc % 2)
        bD2, wD2 = banks.acquire()
        for j in range(4):
            mm(pb[bD2][:, j:j + 1], ldc[:, j * CK:(j + 1) * CK], ones_sb[:, :],
               True, True, waits=(list(wD2) if j == 0 else []),
               inc=(sT if j == 3 else None))
        P.mark("dcolm", sT)
        P.mark(f"ld_free{gc % 2}", sT)
        P.emit(Ss, (lambda e, b=bD2: e.activation(
            dcols[:, 0:4], pb[b][:, 0:4], ACT.Exp, scale=-1.0)),
            waits=[P.w("dcolm"), P.wo_("dcol_free")], consumes=[bD2])
        P.mark("dcol", sS)

    def fast_b2(gc):
        sl2 = gc % 2
        eb = fast_eb[0]
        P.emit(V, (lambda e, sl2=sl2, eb=eb: e.tensor_mul(
            kct[:, sl2 * INNER:(sl2 + 1) * INNER],
            ktokr[:, sl2 * INNER:(sl2 + 1) * INNER], eb)),
            waits=[P.w("eend"), P.wo_(f"um{sl2}")])
        P.mark(f"eend_free{sl2}", sV)
        P.mark("ktok", sV)
        bU, wU = banks.acquire()
        for h in range(H):
            j, r0 = h // 2, 64 * (h % 2)
            mm(pb[bU][r0:r0 + HD, j * HD:(j + 1) * HD],
               kct[:, sl2 * INNER + h * HD:sl2 * INNER + (h + 1) * HD],
               vbt[:, sl2 * INNER + h * HD:sl2 * INNER + (h + 1) * HD],
               True, True, tp=(0, r0),
               waits=(list(wU) + [P.w("ktok"), P.w("kvb")]) if h == 0 else [],
               inc=(sT if h == H - 1 else None))
        P.mark("um", sT)
        P.mark(f"um{sl2}", sT)
        state_chain(bU)
        P.mark("dcol_free", sV)

    def now(sem):
        return (sem, P.ctr[sem])

    recs = {}
    trecs = {}
    fast_eb = [None]

    def rec_of(g):
        return recs.get(g, {})

    def p1a(gc, ci):
        r = recs.setdefault(gc, {})
        bE, wE = banks.acquire()
        mm(pb[bE][:, :], draw_sb[:, ci * CK:(ci + 1) * CK], dtw_sb[:, :],
           True, True,
           waits=list(wE) + [trecs[gc // TCH]["drawc"], P.wo_("w_dt")], inc=sT)
        r["dt2"] = now(sT)
        P.emit(V, (lambda e, b=bE: e.tensor_add(spy[:, :], pb[b][:, :], dtbB[:, :])),
               waits=[r["dt2"], rec_of(gc - 1).get("sp_s"), P.wo_("w_bc"),
                      P.wo_("spy_free2")],
               consumes=[bE])
        spz = now(sV)
        P.emit(Ss, (lambda e: e.activation(spy[:, :], spy[:, :], ACT.Exp)),
               waits=[spz], inc=sS)
        P.emit(Ss, (lambda e: e.activation(spy[:, :], spy[:, :], ACT.Ln, bias=1.0)),
               inc=sS)
        r["sp_s"] = now(sS)
        P.emit(V, (lambda e: e.tensor_mul(ldt[:, :], spy[:, :], baseB[:, :])),
               waits=[r["sp_s"], rec_of(gc - 1).get("packm")])
        r["ld"] = now(sV)
        P.mark("ld2", sV)

    def p2(gc, ci):
        r = recs[gc]
        sl3 = gc % 2
        bKT, wKT = banks.acquire()
        for j in range(4):
            mm(pb[bKT][:, j * CK:(j + 1) * CK],
               kfm[:, j * MAXW + ci * CK:j * MAXW + (ci + 1) * CK],
               ident_sb[:, :], True, True,
               waits=(list(wKT) + [trecs[gc // TCH]["kfm"]]) if j == 0 else [],
               inc=(sT if j == 3 else None))
        r["ktr"] = now(sT)
        P.emit(V, (lambda e, b=bKT, sl3=sl3: e.tensor_copy(
            ktokr[:, sl3 * INNER:(sl3 + 1) * INNER], pb[b][:, :])),
            waits=[r["ktr"], rec_of(gc - 2).get("ktok"),
                   rec_of(gc - 2).get("kv")], consumes=[bKT])
        r["ktokc"] = now(sV)

    def p1b(gc, ci):
        r = recs[gc]
        sl2 = gc % 2
        ldc = ldt[:, :]
        bE2, wE2 = banks.acquire()
        mm(pb[bE2][:, :], msu_sb[:, :], ldc, True, True,
           waits=list(wE2) + [r["ld"]], inc=sT)
        eendm = now(sT)
        P.emit(Ss, (lambda e, b=bE2: e.activation(
            eend[:, :], pb[b][:, :], ACT.Exp, scale=-1.0)),
            waits=[eendm, rec_of(gc - 1).get("eendf")], consumes=[bE2])
        eendk = now(sS)
        evb = sl2 * 4 * 768
        expj = []
        for j in range(4):
            lds = ldc[:, j * CK:(j + 1) * CK]
            b1, w1 = banks.acquire()
            b2, w2 = banks.acquire()
            mm(pb[b1][:, 0:385], lds, p1a_sb[:, :], True, True,
               waits=list(w1) + list(w2) +
                     ([rec_of(gc - 2).get("ym")] if j == 0 else []))
            mm(pb[b2][:, 0:384], lds, p2n_sb[:, :], True, True, inc=sT)
            packm = now(sT)
            ew = ([rec_of(gc - 2).get("ym"), rec_of(gc - 2).get("chain")]
                  if j == 0 else [])
            P.emit(Ss, (lambda e, b=b1, j=j, sl2=sl2: e.activation(
                dcols[:, sl2 * 4 + j:sl2 * 4 + j + 1], pb[b][:, 384:385],
                ACT.Exp, scale=-1.0)),
                waits=[packm] + ew, inc=sS)
            P.emit(Ss, (lambda e, b=b1, j=j, evb=evb: e.activation(
                ev[:, evb + j * 768:evb + j * 768 + 384], pb[b][:, 0:384],
                ACT.Exp, scale=-1.0)), consumes=[b1])
            P.emit(Ss, (lambda e, b=b2, j=j, evb=evb: e.activation(
                ev[:, evb + j * 768 + 384:evb + (j + 1) * 768], pb[b][:, 0:384],
                ACT.Exp, scale=-1.0)), consumes=[b2])
            expj.append(now(sS))
        r["packm"] = now(sT)
        r["dcol"] = now(sS)
        P.emit(V, (lambda e, sl2=sl2: e.tensor_mul(
            kct[:, sl2 * INNER:(sl2 + 1) * INNER],
            ktokr[:, sl2 * INNER:(sl2 + 1) * INNER], eend[:, :])),
            waits=[eendk, r["ktokc"], rec_of(gc - 2).get("um")])
        r["eendf"] = now(sV)
        r["ktok"] = now(sV)
        P.emit(G, (lambda e, sl2=sl2: e.tensor_mul(
            kvprod[:, sl2 * INNER:(sl2 + 1) * INNER],
            ktokr[:, sl2 * INNER:(sl2 + 1) * INNER],
            vraw[:, 0:INNER])),
            waits=[r["vb"], r["ktokc"], rec_of(gc - 2).get("bny")])
        r["kv"] = now(sG)
        kf = kfm[:, ci * CK:ci * CK + CK]
        krep = bass.AP(kf.tensor, kf.offset,
                       [kf.ap[0], [MAXW, 4], [0, 6], kf.ap[-1]])
        P.emit(G, (lambda e, a=krep, evb=evb: e.tensor_mul(
            ev[:, evb:evb + 4 * 768], a, ev[:, evb:evb + 4 * 768])),
            waits=expj + [trecs[gc // TCH]["kfm"], rec_of(gc - 2).get("ym")])
        r["kvar"] = now(sG)

    def p3(ti, gc, ci):
        r = recs[gc]
        sl2 = gc % 2
        sl4 = gc % 4
        tslot = (ti % 2) * TCH + ci
        tr = trecs[ti]
        evb = sl2 * 4 * 768
        abanks = []
        for gq in range(2):
            bi, bw = banks.acquire()
            abanks.append(bi)
            for hh in range(4):
                h = gq * 4 + hh
                j, r0 = h // 2, 64 * (h % 2)
                col0 = hh * CK
                EV = ev[:, evb + j * 768:evb + (j + 1) * 768]
                w0 = (list(bw) + [r["kvar"]]) if hh == 0 else []
                mm(pb[bi][:, col0:col0 + CK], EV[r0:r0 + HD, 640:768],
                   EV[r0:r0 + HD, 512:640], True, True, tp=(r0, 0), waits=w0)
                mm(pb[bi][0:64, col0 + 64:col0 + 128], EV[r0:r0 + HD, 128:192],
                   EV[r0:r0 + HD, 192:256], True, True, tp=(r0, 0))
                mm(pb[bi][0:32, col0 + 32:col0 + 64], EV[r0:r0 + HD, 256:288],
                   EV[r0:r0 + HD, 288:320], True, True, tp=(r0, 0))
                mm(pb[bi][64:96, col0 + 96:col0 + 128], EV[r0:r0 + HD, 320:352],
                   EV[r0:r0 + HD, 352:384], True, True, tp=(r0, 64))
                mm(pb[bi][0:16, col0 + 16:col0 + 32], EV[r0:r0 + HD, 384:400],
                   EV[r0:r0 + HD, 400:416], True, True, tp=(r0, 0))
                mm(pb[bi][32:48, col0 + 48:col0 + 64], EV[r0:r0 + HD, 416:432],
                   EV[r0:r0 + HD, 432:448], True, True, tp=(r0, 32))
                mm(pb[bi][64:80, col0 + 80:col0 + 96], EV[r0:r0 + HD, 448:464],
                   EV[r0:r0 + HD, 464:480], True, True, tp=(r0, 64))
                mm(pb[bi][96:112, col0 + 112:col0 + 128], EV[r0:r0 + HD, 480:496],
                   EV[r0:r0 + HD, 496:512], True, True, tp=(r0, 96),
                   inc=(sT if hh == 3 else None))
        r["am"] = now(sT)
        for gq, bi in enumerate(abanks):
            P.emit(V, (lambda e, b=bi, gq=gq: e.copy_predicated(
                a_sb[:, gq * 4 * CK:(gq * 4 + 4) * CK],
                mask_sb[:, :], pb[b][:, :])),
                waits=[r["am"]] + ([rec_of(gc - 1).get("ym")] if gq == 0 else []),
                consumes=[bi], vhz=True)
        asel = now(sV)
        chain_prev = rec_of(gc - 1).get("chain") or P.w("chain")
        bY, wY = banks.acquire()
        for h in range(H):
            j, r0 = h // 2, 64 * (h % 2)
            kg = ev[:, evb + j * 768:evb + j * 768 + CK]
            w0 = (list(wY) + [chain_prev, asel, r["vb"]]) if h == 0 else []
            mm(pb[bY][:, h * HD:(h + 1) * HD], kg[r0:r0 + HD, :],
               Sbf[r0:r0 + HD, j * HD:(j + 1) * HD], True, False, tp=(r0, 0),
               waits=w0)
            mm(pb[bY][:, h * HD:(h + 1) * HD],
               a_sb[:, h * CK:(h + 1) * CK],
               vbt[:, sl4 * INNER + h * HD:sl4 * INNER + (h + 1) * HD],
               False, True, tp=(0, 0),
               inc=(sT if h == H - 1 else None))
        r["ym"] = now(sT)
        bU, wU = banks.acquire()
        for h in range(H):
            j, r0 = h // 2, 64 * (h % 2)
            mm(pb[bU][r0:r0 + HD, j * HD:(j + 1) * HD],
               kct[:, sl2 * INNER + h * HD:sl2 * INNER + (h + 1) * HD],
               vbt[:, sl4 * INNER + h * HD:sl4 * INNER + (h + 1) * HD],
               True, True, tp=(0, r0),
               waits=(list(wU) + [r["ktok"], r["vb"]]) if h == 0 else [],
               inc=(sT if h == H - 1 else None))
        r["um"] = now(sT)
        P.emit(V, (lambda e, sl2=sl2: e.tensor_mul(
            S_sb[:, :], S_sb[:, :], dcol_rep(sl2 * 4))),
            waits=[r["um"], r["dcol"], r["ym"]])
        P.emit(V, (lambda e, b=bU: e.tensor_add(S_sb[:, :], S_sb[:, :],
                                                pb[b][:, 0:4 * HD])),
               consumes=[bU])
        P.emit(V, lambda e: e.tensor_copy(Sbf[:, :], S_sb[:, :]))
        r["chain"] = now(sV)
        P.mark("chain", sV)
        P.emit(V, (lambda e, b=bY, sl2=sl2: e.tensor_add(
            bny[:, :], kvprod[:, sl2 * INNER:(sl2 + 1) * INNER], pb[b][:, :])),
            waits=[r["ym"], r["kv"]], consumes=[bY])
        r["bny"] = now(sV)
        P.emit(G, (lambda e, tslot=tslot, sl2=sl2: e.tensor_mul(
            ys_sb[:, sl2 * INNER:(sl2 + 1) * INNER],
            gp_sb[:, tslot * INNER:(tslot + 1) * INNER], bny[:, :])),
            waits=[tr["thrg"], r["bny"], rec_of(gc - 2).get("ystr")])
        r["ysb2"] = now(sG)
        r["ys"] = now(sG)

    def p4a(ti, gc, ci):
        r = recs[gc]
        slot = gc % 5
        sl2 = gc % 2
        bYS, wYS = banks.acquire()
        for icb in range(NI):
            mm(pb[bYS][:, icb * CK:(icb + 1) * CK],
               ys_sb[:, sl2 * INNER + icb * CK:sl2 * INNER + (icb + 1) * CK],
               ident_sb[:, :], True, True,
               waits=(list(wYS) + [r["ys"]]) if icb == 0 else [],
               inc=(sT if icb == NI - 1 else None))
        r["ystr"] = now(sT)
        P.emit(Ss, (lambda e, b=bYS: e.copy(ysfm[:, :], pb[b][:, :])),
               waits=[r["ystr"], rec_of(gc - 1).get("wo_done")], consumes=[bYS])
        ysfmr = now(sS)
        for half in range(2):
            bW, wW = banks.acquire()
            for icb in range(NI):
                mm(pb[bW][:, 0:INNER],
                   ysfm[:, icb * CK:(icb + 1) * CK],
                   wo_sb[:, icb * D + half * INNER:icb * D + (half + 1) * INNER],
                   icb == 0, icb == NI - 1,
                   waits=(list(wW) + ([ysfmr, P.wo_("w_o")]
                                      if half == 0 else [])) if icb == 0 else [],
                   inc=(sT if icb == NI - 1 else None))
            wo_m = now(sT)
            xsrc = xch[:, slot * D + half * INNER:slot * D + (half + 1) * INNER]
            dst = x2[:, ci * D + half * INNER:ci * D + (half + 1) * INNER]
            wx = [wo_m, rec_of(gc - 3).get("xln2n")]
            if P.ctr[sDO] > 0:
                wx.append((sDO, P.ctr[sDO]))
            P.emit(V, (lambda e, d=dst, a=xsrc, b=bW: e.tensor_add(
                d, a, pb[b][:, 0:INNER])), waits=wx, consumes=[bW])
        r["wo_done"] = now(sT)
        r["x2"] = now(sV)

    def p4b_stats(ti, gc, ci):
        r = recs[gc]
        lnc2 = ln_stats(x2[:, ci * D:(ci + 1) * D], 4, 5, r["x2"], so=3)
        P.emit(Ss, (lambda e, ci=ci: e.activation(
            xln2[:, :], x2[:, ci * D:(ci + 1) * D],
            ACT.Identity, bias=lncols[:, 5:6], scale=lncols[:, 4:5])),
            waits=[lnc2, P.wo_("ln2tr")], inc=sS)
        r["xln2n"] = now(sS)
        P.mark("xln2n", sS)

    def p4b(ti, gc, ci):
        r = recs[gc]

        def srcT2(jj):
            return xln2[:, jj * CK:(jj + 1) * CK]

        def dstT2(g0, gn, ci=ci):
            rr = x2fm[:, :].rearrange("p (dc w) -> p dc w", dc=ND)
            return rr[:, g0:g0 + gn, 1 + ci * CK:1 + (ci + 1) * CK]

        transpose_group(srcT2, ND, dstT2, [r["xln2n"]],
                        extra_evac_waits=[P.wo_("mix")])
        P.mark("ln2tr", sT)
        r["x2fm"] = now(sS)

    def prologue(ti, gc0, ncint):
        W = ncint * CK
        tr = trecs.setdefault(ti, {})
        last_prev = rec_of(gc0 - 1)
        for ci in range(ncint):
            gc = gc0 + ci
            slot = gc % 5
            ln1_chunk("xm", gc, slot, ci)
        tr["ln1"] = now(sS)
        lfm = [tr["ln1"]]
        for oc in range(NI):
            bi, bw = banks.acquire()
            for dc in range(ND):
                mm(pb[bi][:, 0:W],
                   wk_sb[:, dc * INNER + oc * CK:dc * INNER + (oc + 1) * CK],
                   ln1fm[:, dc * MAXW:dc * MAXW + W], dc == 0, dc == ND - 1,
                   waits=(list(bw) + (lfm +
                          [last_prev.get("ktr"), last_prev.get("kvar")]
                          if oc == 0 else [])) if dc == 0 else [],
                   inc=(sT if dc == ND - 1 else None))
            kfm_m = now(sT)
            P.emit(Ss, (lambda e, oc=oc, W=W, b=bi: e.copy(
                kfm[:, oc * MAXW:oc * MAXW + W], pb[b][:, 0:W])),
                waits=[kfm_m], consumes=[bi])
        tr["kfm"] = now(sS)
        bD, wD = banks.acquire()
        for dc in range(ND):
            mm(pb[bD][0:32, 0:W], wsm_sb[:, dc * 32:(dc + 1) * 32],
               ln1fm[:, dc * MAXW:dc * MAXW + W], dc == 0, dc == ND - 1,
               waits=(list(wD) + ([last_prev.get("dt2")] if dc == 0 else []))
                     if dc == 0 else [],
               inc=(sT if dc == ND - 1 else None))
        drawm = now(sT)
        P.emit(Ss, (lambda e, W=W, b=bD: e.copy(draw_sb[:, 0:W], pb[b][0:32, 0:W])),
               waits=[drawm, last_prev.get("dt2")], consumes=[bD])
        tr["drawc"] = now(sS)

    def rgvb(ti, gc0, ncint):
        tr = trecs[ti]
        for c in range(ncint):
            tslot = (ti % 2) * TCH + c
            bR, wR = banks.acquire()
            bG, wG = banks.acquire()
            for dc in range(ND):
                st, sp = dc == 0, dc == ND - 1
                lhs = ln1fm[:, dc * MAXW + c * CK:dc * MAXW + (c + 1) * CK]
                w0 = (list(wR) + list(wG) +
                      ([P.wo_("w_r"), P.wo_("w_g"), tr["ln1"]]
                       if c == 0 else [])) if dc == 0 else []
                mm(pb[bR][:, :], lhs, wr_sb[:, dc * INNER:(dc + 1) * INNER],
                   st, sp, waits=w0)
                mm(pb[bG][:, :], lhs, wg_sb[:, dc * INNER:(dc + 1) * INNER],
                   st, sp, inc=(sT if sp else None))
            rgm = now(sT)
            P.emit(Ss, (lambda e, b=bR: e.activation(
                rgtmp[:, 0:INNER], pb[b][:, :], ACT.Sigmoid)),
                   waits=[rgm, P.wo_("gpv")], consumes=[bR])
            P.emit(Ss, (lambda e, b=bG: e.activation(
                rgtmp[:, INNER:2 * INNER], pb[b][:, :], ACT.Sigmoid)),
                consumes=[bG])
            gprev = recs.get(gc0 - 2 * TCH + c, {}).get("ysb2")
            P.emit(V, (lambda e, tslot=tslot: e.tensor_mul(
                gp_sb[:, tslot * INNER:(tslot + 1) * INNER], rgtmp[:, 0:INNER],
                rgtmp[:, INNER:2 * INNER])),
                waits=[now(sS), gprev])
            P.mark("gpv", sV)
        tr["thrg"] = now(sV)

    def pvb(gc, ci):
        r = recs[gc]
        sl2 = gc % 2
        sl4 = gc % 4
        bV2, wV2 = banks.acquire()
        bB, wB = banks.acquire()
        for dc in range(ND):
            st, sp = dc == 0, dc == ND - 1
            lhs = ln1fm[:, dc * MAXW + ci * CK:dc * MAXW + (ci + 1) * CK]
            w0 = (list(wV2) + list(wB)) if dc == 0 else []
            mm(pb[bV2][:, :], lhs, wv_sb[:, dc * INNER:(dc + 1) * INNER],
               st, sp, waits=w0)
            mm(pb[bB][:, 0:1], lhs, wbar_sb[:, dc:dc + 1], st, sp,
               inc=(sT if sp else None))
        rsv = now(sT)
        P.emit(V, (lambda e, b=bB: e.tensor_copy(bcol[:, 0:1], pb[b][:, 0:1])),
               waits=[rsv], consumes=[bB])
        P.emit(V, (lambda e, b=bV2: e.tensor_mul(
            vraw[:, 0:INNER], pb[b][:, :], tfB[:, :])))
        P.emit(V, (lambda e, b=bV2, sl4=sl4: e.tensor_scalar(
            vbt[:, sl4 * INNER:(sl4 + 1) * INNER], pb[b][:, :],
            bcol[:, 0:1], None, OP.mult)),
            waits=[rec_of(gc - 4).get("um"), rec_of(gc - 4).get("ym")],
            consumes=[bV2], vhz=True)
        r["vb"] = now(sV)

    def ffn_mixes(ti, gc0, ncint):
        W = ncint * CK
        r_last = recs[gc0 + ncint - 1]
        for dc in range(ND):
            xcur = x2fm[:, dc * (MAXW + 1) + 1:dc * (MAXW + 1) + 1 + W]
            xprv = x2fm[:, dc * (MAXW + 1):dc * (MAXW + 1) + W]
            P.emit(V, (lambda e, a=xcur, b=xprv, W=W: e.tensor_sub(
                mixd[:, 0:W], a, b)),
                waits=([r_last["x2fm"], P.wo_("rffn")]
                       if dc == 0 else []))
            P.emit(V, (lambda e, b=xprv, dc=dc, W=W: e.scalar_tensor_tensor(
                xr_sb[:, dc * MAXW:dc * MAXW + W], mixd[:, 0:W],
                mur_sb[:, dc:dc + 1], b, OP.mult, OP.add)))
            P.emit(V, (lambda e, b=xprv, dc=dc, W=W: e.scalar_tensor_tensor(
                xk_sb[:, dc * MAXW:dc * MAXW + W], mixd[:, 0:W],
                muk_sb[:, dc:dc + 1], b, OP.mult, OP.add)))
        P.emit(V, (lambda e, W=W: e.tensor_copy(
            x2fm[:, :].rearrange("p (dc w) -> p dc w", dc=ND)[:, :, 0:1],
            x2fm[:, :].rearrange("p (dc w) -> p dc w", dc=ND)[:, :, W:W + 1])))
        P.mark("mix", sV)

    def ffn_rest(ti, gc0, ncint):
        W = ncint * CK
        mixw = P.w("mix")
        for oc in range(ND):
            bi, bw = banks.acquire()
            for cc in range(ND):
                mm(pb[bi][:, 0:W],
                   fwkT[:, (cc * ND + oc) * CK:(cc * ND + oc + 1) * CK],
                   xk_sb[:, cc * MAXW:cc * MAXW + W], cc == 0, cc == ND - 1,
                   waits=(list(bw) + ([mixw, P.wo_("w_fk")]
                                      if oc == 0 else [])) if cc == 0 else [],
                   inc=(sT if cc == ND - 1 else None))
            kffn = now(sT)
            P.emit(V, (lambda e, oc=oc, W=W, b=bi: e.tensor_scalar(
                kact[:, oc * MAXW:oc * MAXW + W], pb[b][:, 0:W], 0.0, None,
                OP.max)),
                waits=[kffn] + ([P.wo_("vffn")] if oc == 0 else []),
                consumes=[bi])
            P.emit(Ss, (lambda e, oc=oc, W=W: e.activation(
                kact[:, oc * MAXW:oc * MAXW + W],
                kact[:, oc * MAXW:oc * MAXW + W], ACT.Square)),
                waits=[now(sV)])
        P.mark("kact", sS)
        for ci in range(ncint):
            for half in range(2):
                bR, wR = banks.acquire()
                for cc in range(ND):
                    mm(pb[bR][:, 0:INNER],
                       xr_sb[:, cc * MAXW + ci * CK:cc * MAXW + (ci + 1) * CK],
                       fwrT[:, (cc * ND + half * 4) * CK:(cc * ND + half * 4 + 4) * CK],
                       cc == 0, cc == ND - 1,
                       waits=(list(wR) + ([P.wo_("w_fr")]
                                          if (ci == 0 and half == 0) else []))
                             if cc == 0 else [],
                       inc=(sT if cc == ND - 1 else None))
                P.mark("rffn", sT)
                P.emit(Ss, (lambda e, b=bR: e.activation(
                    spy[:, 0:INNER], pb[b][:, 0:INNER], ACT.Sigmoid)),
                    waits=[P.w("rffn"), P.wo_("spy_free2"), P.wo_("ld2")],
                    consumes=[bR])
                P.mark("thf", sS)
                bV3, wV3 = banks.acquire()
                for cc in range(ND):
                    mm(pb[bV3][:, 0:INNER],
                       kact[:, cc * MAXW + ci * CK:cc * MAXW + (ci + 1) * CK],
                       fwvT[:, (cc * ND + half * 4) * CK:(cc * ND + half * 4 + 4) * CK],
                       cc == 0, cc == ND - 1,
                       waits=(list(wV3) + ([P.w("kact"), P.wo_("w_fv")]
                                           if (ci == 0 and half == 0) else []))
                             if cc == 0 else [],
                       inc=(sT if cc == ND - 1 else None))
                P.mark("vffn", sT)
                P.emit(V, (lambda e, b=bV3: e.tensor_mul(
                    spy[:, 0:INNER], spy[:, 0:INNER], pb[b][:, 0:INNER])),
                    waits=[P.w("vffn"), P.w("thf")],
                    consumes=[bV3])
                P.mark("ffnt", sV)
                dst = x2[:, ci * D + half * INNER:ci * D + (half + 1) * INNER]
                P.emit(V, (lambda e, d=dst: e.tensor_add(d, d, spy[:, 0:INNER])))
                P.mark("spy_free2", sV)
        x3 = now(sV)
        for ci in range(ncint):
            gc = gc0 + ci
            if gc == 0:
                continue
            src2 = x2[:, ci * D:(ci + 1) * D]
            dst = out[(gc - 1) * CK:gc * CK, :]
            P.emit(SY, (lambda e, d=dst, s=src2: e.dma_start(d, s)),
                   waits=[x3], inc=sDO, inc_n=16)

    FST = 2
    dma_x(xf, "xf", FST, FST % 3, waits=[(sDC, cdone)])
    dma_x(xf, "xf", FST + 1, (FST + 1) % 3)
    fast_ln1a(FST)
    fast_ln1b(FST)
    fast_ln1a(FST + 1)
    fast_ln1b(FST + 1)
    _fb = fast_projMM(FST)
    fast_projrest(FST, *_fb)
    _fb = fast_projMM(FST + 1)
    fast_projrest(FST + 1, *_fb)
    fast_dt_a(FST)
    dt_path_b(FST)
    for g in range(FST, NCH):
        fast_b1(g)
        fast_b2(g)
        if g + 1 < NCH:
            fast_dt_a(g + 1)
        if g + 2 < NCH:
            fast_ln1a(g + 2)
        if g + 1 < NCH:
            dt_path_b(g + 1)
        if g + 2 < NCH:
            fast_ln1b(g + 2)
            _fb = fast_projMM(g + 2)
            fast_projrest(g + 2, *_fb)
        pump_loader(2)
    pump_loader(1000)

    for gseed in range(3):
        dma_x(xm, "xm", gseed, gseed % 5, waits=[P.wo_("xln1n")])

    snap = [(s, P.ctr[s]) for s in (sT, sV, sS)]
    P.barrier((T, V, Ss, G), snap)

    for g in range(NCH + 4):
        if 0 <= g - 4 < NCH:
            go = g - 4
            p4b(go // TCH, go, go % TCH)
            if go % TCH == TCH - 1:
                to = go // TCH
                ffn_mixes(to, to * TCH, TCH)
                ffn_rest(to, to * TCH, TCH)
        if 0 <= g - 2 < NCH:
            go = g - 2
            p3(go // TCH, go, go % TCH)
        if 0 <= g - 3 < NCH:
            go = g - 3
            p4a(go // TCH, go, go % TCH)
            p4b_stats(go // TCH, go, go % TCH)
        if 1 <= g < NCH and g + 2 < NCH:
            dma_x(xm, "xm", g + 2, (g + 2) % 5,
                  waits=[rec_of(g - 3).get("x2"), P.wo_("xln1n")])
        if g < NCH:
            ti, ci = g // TCH, g % TCH
            if ci == 0:
                prologue(ti, ti * TCH, TCH)
                rgvb(ti, ti * TCH, TCH)
            p1a(g, ci)
            pvb(g, ci)
            p2(g, ci)
            p1b(g, g % TCH)

    P.emit(G, lambda e: e.memset(bcol[:, 1:2], 0.0), waits=[(sDO, P.ctr[sDO])])

    P.finalize()
    stack.close()
    return nc


def kernel(**inputs):
    x = np.asarray(inputs["L0_attn_x"])
    consts = host_constants()
    nc = build_program()

    base = np.asarray(inputs["L0_attn_base_decay"]).reshape(1, INNER)
    dtb = np.asarray(inputs["L0_attn_dt_b"]).reshape(1, INNER)
    tf = np.asarray(inputs["L0_attn_time_first"]).reshape(1, INNER)
    mur_h = np.ascontiguousarray(
        np.asarray(inputs["L0_ffn_mu_r"]).reshape(ND, CK).T, dtype=np.float32)
    muk_h = np.ascontiguousarray(
        np.asarray(inputs["L0_ffn_mu_k"]).reshape(ND, CK).T, dtype=np.float32)

    common = dict(
        wsel=np.ascontiguousarray(inputs["L0_attn_W_sel"], dtype=np.float32),
        dtw=np.ascontiguousarray(inputs["L0_attn_dt_W"], dtype=np.float32),
        wk=np.ascontiguousarray(inputs["L0_attn_Wk"], dtype=np.float32),
        wv=np.ascontiguousarray(inputs["L0_attn_Wv"], dtype=np.float32),
        wr=np.ascontiguousarray(inputs["L0_attn_Wr"], dtype=np.float32),
        wg=np.ascontiguousarray(inputs["L0_attn_Wg"], dtype=np.float32),
        wo=np.ascontiguousarray(inputs["L0_attn_Wo"], dtype=np.float32),
        fwr=np.ascontiguousarray(inputs["L0_ffn_W_r"], dtype=np.float32),
        fwk=np.ascontiguousarray(inputs["L0_ffn_W_k"], dtype=np.float32),
        fwv=np.ascontiguousarray(inputs["L0_ffn_W_v"], dtype=np.float32),
        dtb_b=np.ascontiguousarray(np.broadcast_to(dtb, (CK, INNER)),
                                   dtype=np.float32),
        base_b=np.ascontiguousarray(np.broadcast_to(base, (CK, INNER)),
                                    dtype=np.float32),
        tf_b=np.ascontiguousarray(np.broadcast_to(tf, (CK, INNER)),
                                  dtype=np.float32),
        mur=mur_h,
        muk=muk_h,
        **consts,
    )

    in_maps = []
    for c in range(8):
        b, h = c // 2, c % 2
        T0 = 4096 * h
        xmain = np.zeros((ROWS, D), np.float32)
        if h == 0:
            xmain[CK:] = x[b, 0:4096]
        else:
            xmain[:] = x[b, T0 - CK:T0 + 4096]
        xfast = np.zeros((ROWS, D), np.float32)
        if h == 1:
            xfast[2 * CK:] = x[b, 0:4096 - CK]
        m = dict(common)
        m["xm"] = xmain
        m["xf"] = xfast
        in_maps.append(m)

    res = run_bass_kernel_spmd(nc, in_maps, core_ids=list(range(8)))
    y = np.empty((4, 8192, D), np.float32)
    for c in range(8):
        b, h = c // 2, c % 2
        y[b, 4096 * h:4096 * (h + 1)] = np.asarray(res.results[c]["out"])
    return y


# version 42 (final)
# speedup vs baseline: 1.0089x; 1.0089x over previous
import numpy as np
import ml_dtypes
import concourse.bass as bass
import concourse.mybir as mybir
from concourse.bass_utils import run_bass_kernel_spmd

F32 = mybir.dt.float32
BF16 = mybir.dt.bfloat16
ACT = mybir.ActivationFunctionType
OP = mybir.AluOpType
AXL = mybir.AxisListType

BF = ml_dtypes.bfloat16

D = 1024
ND = 8
NI = 4
INNER = 512
H = 8
HD = 64
CK = 128
EPS = 1e-5

TCH = 3
MAXW = TCH * CK
TILES = (TCH,) * 11
NCH = sum(TILES)
ROWS = NCH * CK
OUT_ROWS = (NCH - 1) * CK

ENGINES = ("tensor", "vector", "scalar", "sync", "gpsimd")


def host_constants():
    ident = np.eye(CK, dtype=np.float32).astype(BF)
    tri = np.triu(np.ones((CK, CK), np.float32))
    msu = np.tril(np.ones((CK, CK), np.float32), -1)

    def row_lvl(anchor):
        C = np.zeros((CK, CK), np.float32)
        for t in range(CK):
            B = anchor(t)
            if B is None:
                continue
            if t >= B:
                C[B:t + 1, t] = 1.0
            else:
                C[t + 1:B, t] = -1.0
        return C

    def col_lvl(anchor):
        C = np.zeros((CK, CK), np.float32)
        for s in range(CK):
            B = anchor(s)
            if B is None:
                continue
            if s < B:
                C[s + 1:B, s] = 1.0
            else:
                C[B:s + 1, s] = -1.0
        return C

    r64 = row_lvl(lambda t: 64 if t >= 64 else None)
    r32 = row_lvl(lambda t: 32 if 32 <= t < 64 else (96 if t >= 96 else None))
    r16 = row_lvl(lambda t: {1: 16, 3: 48, 5: 80, 7: 112}.get(t // 16))
    rdg = row_lvl(lambda t: 16 * (t // 16) + 8)
    c64 = col_lvl(lambda s: 64 if s < 64 else None)
    c32 = col_lvl(lambda s: 32 if s < 32 else (96 if 64 <= s < 96 else None))
    c16 = col_lvl(lambda s: {0: 16, 2: 48, 4: 80, 6: 112}.get(s // 16))
    cdg = col_lvl(lambda s: 16 * (s // 16) + 8)

    p1a = np.concatenate([tri, r64 + c64, r32 + c32,
                          np.ones((CK, 1), np.float32)], 1).astype(BF)
    p2n = np.concatenate([r16 + c16, rdg, cdg], 1).astype(BF)
    mask4 = np.tile(np.triu(np.ones((CK, CK), np.uint8)), (1, 4))
    onesc = np.ones((CK, 1), np.float32).astype(BF)
    fmisc = np.tile(np.array([[EPS, 1.0, -1.0, 0.5, -0.5, 0.0, 2.0, 0.25]],
                             np.float32), (CK, 1))
    return dict(cident=ident, cp1a=p1a, cp2n=p2n,
                cmsu=msu.astype(BF), cmask=mask4, conesc=onesc, cfmisc=fmisc)


class Prog:
    def __init__(self, nc, stack):
        self.nc = nc
        self.stack = stack
        self.streams = {e: [] for e in ENGINES}
        self.sems = {}
        self.ctr = {}
        self.marks = {}
        self.pending = {}
        self.autosem = {}

    def sem(self, name):
        h = self.stack.enter_context(self.nc.semaphore(name))
        self.sems[name] = h
        self.ctr[name] = 0
        return name

    def emit(self, eng, fn, waits=(), inc=None, inc_n=1, consumes=(), vhz=False):
        ws = []
        pend = self.pending.pop(eng, None)
        if pend:
            waits = list(waits) + pend
        auto = self.autosem.get(eng)
        if auto is not None:
            if self.ctr[auto] > 0:
                waits = list(waits) + [(auto, self.ctr[auto])]
            if inc is None:
                inc = auto
                inc_n = 1
        for wv in waits:
            if wv is None:
                continue
            s, v = wv
            assert v <= self.ctr[s], (s, v, self.ctr[s])
            if v > 0:
                ws.append((s, v))
        wd = {}
        for s, v in ws:
            wd[s] = max(wd.get(s, 0), v)
        ws = list(wd.items())
        if inc is not None:
            self.ctr[inc] += inc_n
        self.streams[eng].append((fn, tuple(ws), (inc, inc_n) if inc else None))
        for bi in consumes:
            assert inc is not None
            self.banks.note(bi, inc)

    def barrier(self, engines, waits):
        for e in engines:
            self.pending.setdefault(e, []).extend(waits)

    def mark(self, name, sem):
        self.marks[name] = (sem, self.ctr[sem])

    def w(self, name):
        return self.marks[name]

    def wo_(self, name):
        return self.marks.get(name)

    def finalize(self):
        nc = self.nc
        own_sem = {"tensor": ("sT", 0), "vector": ("sV", 1),
                   "scalar": ("sS", 1), "gpsimd": ("sG", 1)}
        with nc.Block() as block:
            def run_stream(eng_name):
                own, strict = own_sem.get(eng_name, (None, 0))
                can_attach = eng_name in ("tensor", "vector", "scalar")
                def body(e):
                    wm = {}
                    own_cnt = 0
                    for fn, ws, inc in self.streams[eng_name]:
                        attach = None
                        standalone = []
                        for s, v in ws:
                            if s == own and v + strict <= own_cnt:
                                continue
                            if wm.get(s, 0) >= v:
                                continue
                            standalone.append((s, v))
                            wm[s] = v
                        if attach is None and standalone and can_attach:
                            attach = standalone.pop()
                        for s, v in standalone:
                            e.wait_ge(self.sems[s], v)
                        inst = fn(e)
                        if attach is not None:
                            inst._wait_ge(self.sems[attach[0]], attach[1])
                        if inc is not None:
                            inst.then_inc(self.sems[inc[0]], inc[1])
                            if inc[0] == own:
                                own_cnt += inc[1]
                return body
            block.tensor(run_stream("tensor"))
            block.vector(run_stream("vector"))
            block.scalar(run_stream("scalar"))
            block.sync(run_stream("sync"))
            block.gpsimd(run_stream("gpsimd"))


class Banks:
    def __init__(self, p, tensors):
        self.p = p
        self.t = tensors
        self.waits = [[] for _ in tensors]
        self.rr = 0

    def acquire(self):
        i = self.rr % len(self.t)
        self.rr += 1
        w = self.waits[i]
        self.waits[i] = []
        return i, list(w)

    def note(self, i, sem):
        self.waits[i].append((sem, self.p.ctr[sem]))


def build_program():
    from contextlib import ExitStack
    nc = bass.Bass("TRN2", target_bir_lowering=False)
    stack = ExitStack()

    def din(name, shape, dt=F32):
        return nc.dram_tensor(name, list(shape), dt, kind="ExternalInput")

    xm = din("xm", (ROWS, D))
    xf = din("xf", (ROWS, D))
    out = nc.dram_tensor("out", [OUT_ROWS, D], F32, kind="ExternalOutput")

    wsel = din("wsel", (D, 160))
    dtw = din("dtw", (32, INNER))
    wk = din("wk", (D, INNER))
    wv = din("wv", (D, INNER))
    wr = din("wr", (D, INNER))
    wg = din("wg", (D, INNER))
    wo = din("wo", (INNER, D))
    fwr = din("fwr", (D, D))
    fwk = din("fwk", (D, D))
    fwv = din("fwv", (D, D))
    dtb_b = din("dtb_b", (CK, INNER))
    base_b = din("base_b", (CK, INNER))
    tf_b = din("tf_b", (CK, INNER))
    mur = din("mur", (CK, ND))
    muk = din("muk", (CK, ND))
    cident = din("cident", (CK, CK), BF16)
    cp1a = din("cp1a", (CK, 385), BF16)
    cp2n = din("cp2n", (CK, 384), BF16)
    cmsu = din("cmsu", (CK, CK), BF16)
    cmask = din("cmask", (CK, 4 * CK), mybir.dt.uint8)
    conesc = din("conesc", (CK, 1), BF16)
    cfmisc = din("cfmisc", (CK, 8), F32)

    def s_(name, shape, dt):
        return stack.enter_context(nc.sbuf_tensor(name, list(shape), dt))

    wk_sb = s_("wk_sb", (CK, ND * INNER), BF16)
    wv_sb = s_("wv_sb", (CK, ND * INNER), BF16)
    wr_sb = s_("wr_sb", (CK, ND * INNER), BF16)
    wg_sb = s_("wg_sb", (CK, ND * INNER), BF16)
    wo_sb = s_("wo_sb", (CK, NI * D), BF16)
    fwrT = s_("fwrT", (CK, 64 * CK), BF16)
    fwkT = s_("fwkT", (CK, 64 * CK), BF16)
    fwvT = s_("fwvT", (CK, 64 * CK), BF16)
    wsm_sb = s_("wsm_sb", (CK, ND * 32), BF16)
    wbar_sb = s_("wbar_sb", (CK, ND), BF16)
    dtw_sb = s_("dtw_sb", (32, INNER), BF16)

    ident_sb = s_("ident_sb", (CK, CK), BF16)
    p1a_sb = s_("p1a_sb", (CK, 385), BF16)
    p2n_sb = s_("p2n_sb", (CK, 384), BF16)
    msu_sb = s_("msu_sb", (CK, CK), BF16)
    mask_sb = s_("mask_sb", (CK, 4 * CK), mybir.dt.uint8)
    ones_sb = s_("ones_sb", (CK, 1), BF16)
    dtbB = s_("dtbB", (CK, INNER), BF16)
    baseB = s_("baseB", (CK, INNER), BF16)
    tfB = s_("tfB", (CK, INNER), BF16)
    mur_sb = s_("mur_sb", (CK, ND), F32)
    muk_sb = s_("muk_sb", (CK, ND), F32)
    fmisc_sb = s_("fmisc_sb", (CK, 8), F32)


    xch = s_("xch", (CK, 5 * D), F32)
    xln2 = s_("xln2", (CK, D), BF16)

    stats = s_("stats", (CK, 64), F32)
    lncols = s_("lncols", (CK, 8), F32)
    xln = s_("xln", (CK, D), BF16)
    ln1fm = s_("ln1fm", (CK, ND * MAXW), BF16)
    kfm = s_("kfm", (CK, NI * MAXW), BF16)
    draw_sb = s_("draw_sb", (32, MAXW), BF16)
    spy = s_("spy", (CK, INNER), BF16)
    ldt = s_("ldt", (CK, INNER), BF16)
    ev = s_("ev", (CK, 2 * 4 * 768), BF16)
    dcols = s_("dcols", (CK, 8), F32)
    eend = s_("eend", (CK, INNER), BF16)
    ktokr = s_("ktokr", (CK, 2 * INNER), BF16)
    kct = s_("kct", (CK, 2 * INNER), BF16)
    vraw = s_("vraw", (CK, INNER), BF16)
    kvprod = s_("kvprod", (CK, 2 * INNER), BF16)
    vbt = s_("vbt", (CK, 4 * INNER), BF16)
    gp_sb = s_("gp_sb", (CK, 6 * INNER), BF16)
    a_sb = s_("a_sb", (CK, H * CK), BF16)
    S_sb = s_("S_sb", (CK, 4 * HD), F32)
    Sbf = s_("Sbf", (CK, 4 * HD), BF16)
    ys_sb = s_("ys_sb", (CK, 2 * INNER), BF16)
    bny = s_("bny", (CK, INNER), BF16)
    ysfm = s_("ysfm", (CK, NI * CK), BF16)
    bcol = s_("bcol", (CK, 2), F32)
    x2 = s_("x2", (CK, TCH * D), F32)
    x2fm = s_("x2fm", (CK, ND * (MAXW + 1)), BF16)
    xr_sb = s_("xr_sb", (CK, ND * MAXW), BF16)
    mixd = s_("mixd", (CK, MAXW), BF16)
    xk_sb = s_("xk_sb", (CK, ND * MAXW), BF16)
    kact = s_("kact", (CK, ND * MAXW), BF16)

    rgtmp = xln
    stg = x2[:, 0:INNER]
    stg2 = x2[:, INNER:2 * INNER]
    wnat = xr_sb[:, 0:INNER]

    P = Prog(nc, stack)
    pb = [stack.enter_context(nc.psum_tensor(f"pb{i}", [CK, INNER], F32))
          for i in range(8)]
    banks = Banks(P, pb)
    P.banks = banks

    sT = P.sem("sT")
    sV = P.sem("sV")
    sS = P.sem("sS")
    sG = P.sem("sG")
    sDX = P.sem("sDX")
    sDW = P.sem("sDW")
    sDW1 = P.sem("sDW1")
    sDC = P.sem("sDC")
    sDO = P.sem("sDO")

    T, V, Ss, SY, G = "tensor", "vector", "scalar", "sync", "gpsimd"
    P.autosem = {V: sV, Ss: sS, G: sG}

    def mm(psl, lhs, rhs, start, stop, tp=None, waits=(), inc=None):
        P.emit(T, (lambda e, psl=psl, lhs=lhs, rhs=rhs, st=start, sp=stop, tp=tp:
                   e.matmul(psl, lhs, rhs, start=st, stop=sp, tile_position=tp)),
               waits=waits, inc=inc)

    nw = {sDW: 0, sDW1: 0}

    def dma_w(dst_ap, src_ap, waits=(), sem=None):
        sem = sDW if sem is None else sem
        nw[sem] += 1
        P.emit(G, (lambda e, d=dst_ap, s=src_ap: e.dma_start(d, s)),
               waits=waits, inc=sem, inc_n=16)
        return (sem, 16 * nw[sem])

    ncst = 0
    for dst, src in ((ident_sb, cident), (p1a_sb, cp1a), (p2n_sb, cp2n),
                     (msu_sb, cmsu),
                     (mask_sb, cmask), (ones_sb, conesc), (mur_sb, mur),
                     (muk_sb, muk), (fmisc_sb, cfmisc)):
        ncst += 1
        P.emit(SY, (lambda e, d=dst[:, :], s=src[:, :]: e.dma_start(d, s)),
               inc=sDC, inc_n=16)
    cdone = 16 * ncst
    for col, val in enumerate((EPS, 1.0, -1.0, 0.5, -0.5, 0.0, 2.0, 0.25)):
        nc.const_aps.register(F32, val, fmisc_sb[:, col:col + 1])
    P.emit(V, lambda e: e.memset(a_sb[:, :], 0.0))
    P.emit(V, lambda e: e.memset(S_sb[:, :], 0.0))
    P.emit(V, lambda e: e.memset(Sbf[:, :], 0.0))
    P.emit(V, lambda e: e.memset(x2fm[:, :], 0.0), inc=sV)
    P.mark("setup_v", sV)

    stgs = [(stg, sDW), (stg2, sDW1)]
    stg_i = [0]

    def stg_next():
        i = stg_i[0] % 2
        stg_i[0] += 1
        t, sem = stgs[i]
        return t, f"stg{i}_free", sem

    def load_plan():
        for cd, dst in ((dtb_b, dtbB), (base_b, baseB), (tf_b, tfB)):
            sg, sfree, ssem = stg_next()
            mark = dma_w(sg[:, :], cd[:, :], waits=[P.wo_(sfree)], sem=ssem)
            P.emit(V, (lambda e, d=dst, sg=sg: e.tensor_copy(d[:, :], sg[:, :])),
                   waits=[mark], inc=sV)
            P.mark(sfree, sV)
            yield
        P.mark("w_bc", sV)
        src = wsel[:, :].rearrange("(dc p) o -> p dc o", p=CK)
        for q4 in range(4):
            sg, sfree, ssem = stg_next()
            mark = dma_w(sg[:, 0:2 * 160], src[:, 2 * q4:2 * q4 + 2, :],
                         waits=[P.wo_(sfree)], sem=ssem)
            P.emit(V, (lambda e, q4=q4, sg=sg: e.tensor_copy(
                wsm_sb[:, :].rearrange("p (dc o) -> p dc o", dc=ND)[
                    :, 2 * q4:2 * q4 + 2, :],
                sg[:, 0:2 * 160].rearrange("p (dc o) -> p dc o", dc=2)[:, :, 0:32])),
                waits=[mark], inc=sV)
            for q in range(2):
                dc = 2 * q4 + q
                P.emit(V, (lambda e, q=q, sg=sg: e.reduce_sum(
                    lncols[:, 6:7], sg[:, q * 160 + 32:q * 160 + 96], axis=AXL.X)),
                    inc=sV, vhz=True)
                P.emit(V, (lambda e, dc=dc: e.tensor_scalar(
                    wbar_sb[:, dc:dc + 1], lncols[:, 6:7], 1.0 / 64, None, OP.mult)),
                    inc=sV, vhz=True)
            P.mark(sfree, sV)
            yield
        sg, sfree, ssem = stg_next()
        mark = dma_w(sg[0:32, 0:INNER], dtw[:, :], waits=[P.wo_(sfree)], sem=ssem)
        P.emit(V, (lambda e, sg=sg: e.tensor_copy(dtw_sb[:, :], sg[0:32, 0:INNER])),
               waits=[mark], inc=sV)
        P.mark(sfree, sV)
        P.mark("w_dt", sV)
        yield
        for wdram, wsb, name in ((wk, wk_sb, "w_k"), (wv, wv_sb, "w_v"),
                                 (wr, wr_sb, "w_r"), (wg, wg_sb, "w_g")):
            for dc in range(ND):
                sg, sfree, ssem = stg_next()
                mark = dma_w(sg[:, :], wdram[dc * CK:(dc + 1) * CK, :],
                             waits=[P.wo_(sfree)], sem=ssem)
                dst = wsb[:, dc * INNER:(dc + 1) * INNER]
                P.emit(V, (lambda e, d=dst, sg=sg: e.tensor_copy(d, sg[:, :])),
                       waits=[mark], inc=sV)
                P.mark(sfree, sV)
                yield
            P.mark(name, sV)
        for ic in range(NI):
            for hf in range(2):
                sg, sfree, ssem = stg_next()
                mark = dma_w(sg[:, :],
                             wo[ic * CK:(ic + 1) * CK, hf * INNER:(hf + 1) * INNER],
                             waits=[P.wo_(sfree)], sem=ssem)
                dst = wo_sb[:, ic * D + hf * INNER:ic * D + (hf + 1) * INNER]
                P.emit(V, (lambda e, d=dst, sg=sg: e.tensor_copy(d, sg[:, :])),
                    waits=[mark], inc=sV)
                P.mark(sfree, sV)
                yield
        P.mark("w_o", sV)
        for wdram, wsbT, scale, name in ((fwr, fwrT, None, "w_fr"),
                                         (fwk, fwkT, None, "w_fk"),
                                         (fwv, fwvT, None, "w_fv")):
            for rb in range(ND):
                for ch in range(2):
                    sg, sfree, ssem = stg_next()
                    mark = dma_w(sg[:, :],
                                 wdram[rb * CK:(rb + 1) * CK,
                                       ch * INNER:(ch + 1) * INNER],
                                 waits=[P.wo_(sfree)], sem=ssem)
                    if scale is None:
                        P.emit(V, (lambda e, sg=sg: e.tensor_copy(wnat[:, :], sg[:, :])),
                               waits=[mark, P.wo_("wnat_free")], inc=sV)
                    else:
                        P.emit(V, (lambda e, sc=scale, sg=sg: e.tensor_scalar(
                            wnat[:, :], sg[:, :], sc, None, OP.mult)),
                            waits=[mark, P.wo_("wnat_free")], inc=sV)
                    P.mark(sfree, sV)
                    P.mark("wnat", sV)
                    bi, bw = banks.acquire()
                    for q in range(4):
                        w0 = (list(bw) + ([P.w("wnat"), (sDC, cdone)]
                                          if q == 0 else []))
                        mm(pb[bi][:, q * CK:(q + 1) * CK],
                           wnat[:, q * CK:(q + 1) * CK], ident_sb[:, :],
                           True, True, waits=w0,
                           inc=(sT if q == 3 else None))
                    P.mark("wtr", sT)
                    P.mark("wnat_free", sT)
                    dst = wsbT[:, :].rearrange(
                        "p (cc rb j) -> p cc rb j", cc=ND, rb=ND)[
                        :, ch * 4:(ch + 1) * 4, rb, :]
                    P.emit(Ss, (lambda e, d=dst, s2=pb[bi][:, 0:4 * CK]:
                                e.copy(d, s2)),
                           waits=[P.w("wtr")], consumes=[bi])
                    yield
            P.mark(name, sS)

    loader = load_plan()
    loads_left = [True]

    def pump_loader(n=1):
        for _ in range(n):
            if not loads_left[0]:
                return
            try:
                next(loader)
            except StopIteration:
                loads_left[0] = False

    pump_loader(3 + 4 + 1 + 8 + 8)

    ndx = [0]
    xmark = {}

    def dma_x(xsrc, key, gchunk, slot, waits=()):
        ndx[0] += 1
        src = xsrc[gchunk * CK:(gchunk + 1) * CK, :]
        P.emit(SY, (lambda e, s=src, sl=slot: e.dma_start(
            xch[:, sl * D:(sl + 1) * D], s)), waits=waits, inc=sDX, inc_n=16)
        xmark[(key, gchunk)] = 16 * ndx[0]

    def xwait(key, gchunk):
        return (sDX, xmark[(key, gchunk)])

    def ln_stats(xap, rcol, ncol, xwait, so=0):
        s0 = 16 * so
        rsm = f"rstd{so}"
        P.emit(V, (lambda e, a=xap, s0=s0: e.bn_stats(
            stats[:, s0:s0 + 6], a[:, 0:INNER])),
               waits=[xwait, P.wo_(rsm)], inc=sV)
        P.emit(V, (lambda e, a=xap, s0=s0: e.bn_stats(
            stats[:, s0 + 6:s0 + 12], a[:, INNER:D])),
               inc=sV)
        P.emit(V, (lambda e, s0=s0: e.bn_aggr(stats[:, s0 + 12:s0 + 14],
                                              stats[:, s0:s0 + 12])), inc=sV)
        P.mark("aggr", sV)
        P.emit(Ss, (lambda e, s0=s0: e.activation(
            stats[:, s0 + 14:s0 + 15], stats[:, s0 + 13:s0 + 14],
            ACT.Ln, bias=EPS)),
               waits=[P.w("aggr")], inc=sS)
        P.emit(Ss, (lambda e, rc=rcol, s0=s0: e.activation(
            lncols[:, rc:rc + 1], stats[:, s0 + 14:s0 + 15],
            ACT.Exp, scale=-0.5)), inc=sS)
        P.mark(rsm, sS)
        P.emit(V, (lambda e, rc=rcol, ncc=ncol, s0=s0: e.tensor_scalar(
            lncols[:, ncc:ncc + 1], stats[:, s0 + 12:s0 + 13],
            lncols[:, rc:rc + 1], -1.0,
            OP.mult, OP.mult)),
            waits=[P.w(rsm), P.wo_("xln1n"), P.wo_("xln2n")], inc=sV)
        P.mark("lncols", sV)
        return now(sV)

    def transpose_group(src_fn, n, dst_fn, waits, extra_evac_waits=()):
        first = True
        for g0 in range(0, n, 4):
            gn = min(4, n - g0)
            bi, bw = banks.acquire()
            for j in range(gn):
                w0 = list(bw) + (list(waits) if (first and j == 0) else [])
                first = False
                mm(pb[bi][:, j * CK:(j + 1) * CK], src_fn(g0 + j),
                   ident_sb[:, :], True, True, waits=w0,
                   inc=(sT if j == gn - 1 else None))
            P.mark("trg", sT)
            dst = dst_fn(g0, gn)
            src2 = pb[bi][:, 0:gn * CK]
            P.emit(Ss, (lambda e, d=dst, s=src2: e.copy(d, s)),
                   waits=[P.w("trg")] + list(extra_evac_waits), consumes=[bi])

    LNP = ((0, 1), (2, 3), (6, 7))

    def ln1_chunk(key, gc, slot, ci):
        assert (key, gc) in xmark
        rcol, ncol = LNP[ci % 3]
        lnc = ln_stats(xch[:, slot * D:(slot + 1) * D], rcol, ncol,
                       xwait(key, gc), so=ci % 3)
        P.emit(Ss, (lambda e, sl=slot, rc=rcol, ncc=ncol: e.activation(
            xln[:, :], xch[:, sl * D:(sl + 1) * D],
            ACT.Identity, bias=lncols[:, ncc:ncc + 1],
            scale=lncols[:, rc:rc + 1])),
            waits=[lnc, P.wo_("ln1tr")], inc=sS)
        P.mark("xln1n", sS)

        def srcT(j):
            return xln[:, j * CK:(j + 1) * CK]

        def dstT(g0, gn, ci=ci):
            r = ln1fm[:, :].rearrange("p (dc w) -> p dc w", dc=ND)
            return r[:, g0:g0 + gn, ci * CK:(ci + 1) * CK]

        transpose_group(srcT, ND, dstT,
                        [P.w("xln1n"), (sDC, cdone)],
                        extra_evac_waits=[P.wo_("ln1fm_free")])
        P.mark("ln1tr", sT)
        P.mark("ln1fm_c", sS)

    def dt_path_a(lhs_draw):
        bE, wE = banks.acquire()
        mm(pb[bE][:, :], lhs_draw, dtw_sb[:, :], True, True,
           waits=list(wE) + [P.w("drawc"), P.wo_("w_dt")], inc=sT)
        P.mark("dt2", sT)
        P.emit(V, (lambda e, b=bE: e.tensor_add(spy[:, :], pb[b][:, :], dtbB[:, :])),
               waits=[P.w("dt2"), P.wo_("sp_s"), P.wo_("w_bc"),
                      P.wo_("spy_free2")], consumes=[bE])
        P.mark("spz", sV)
        P.emit(Ss, (lambda e: e.activation(spy[:, :], spy[:, :], ACT.Exp)),
               waits=[P.w("spz")], inc=sS)
        P.emit(Ss, (lambda e: e.activation(spy[:, :], spy[:, :], ACT.Ln, bias=1.0)),
               inc=sS)
        P.mark("sp_s", sS)

    def dt_path_b(gc=0):
        lds = ldt[:, :] if gc % 2 == 0 else vraw[:, 0:INNER]
        P.emit(V, (lambda e, lds=lds: e.tensor_mul(lds, spy[:, :], baseB[:, :])),
               waits=[P.w("sp_s"), P.wo_(f"ld_free{gc % 2}")], inc=sV)
        P.mark("ld", sV)

    def eend_path(ldc, sl2=0):
        eb = eend[:, :] if sl2 == 0 else kvprod[:, 0:INNER]
        bE2, wE2 = banks.acquire()
        mm(pb[bE2][:, :], msu_sb[:, :], ldc, True, True,
           waits=list(wE2) + [P.w("ld")], inc=sT)
        P.mark("eendm", sT)
        P.emit(Ss, (lambda e, b=bE2, eb=eb: e.activation(
            eb, pb[b][:, :], ACT.Exp, scale=-1.0)),
            waits=[P.w("eendm"), P.wo_(f"eend_free{sl2}")], consumes=[bE2])
        P.mark("eend", sS)
        return eb

    def dcol_rep(c0):
        b = dcols[:, c0:c0 + 4]
        return bass.AP(b.tensor, b.offset, [b.ap[0], b.ap[-1], [0, HD]])

    def state_chain(bU, extra=()):
        P.emit(V, (lambda e: e.tensor_mul(S_sb[:, :], S_sb[:, :], dcol_rep(0))),
               waits=[P.w("um"), P.w("dcol")] + list(extra))
        P.emit(V, (lambda e, b=bU: e.tensor_add(S_sb[:, :], S_sb[:, :],
                                                pb[b][:, 0:4 * HD])),
               consumes=[bU])
        P.emit(V, lambda e: e.tensor_copy(Sbf[:, :], S_sb[:, :]), inc=sV)
        P.mark("chain", sV)

    def fast_ln1a(gc):
        if gc + 2 < NCH:
            dma_x(xf, "xf", gc + 2, (gc + 2) % 3, waits=[P.wo_("xln1n")])
        slot = gc % 3
        ln_stats(xch[:, slot * D:(slot + 1) * D], 0, 1, xwait("xf", gc), so=0)

    def fast_ln1b(gc):
        slot = gc % 3
        ci = gc % 2
        P.emit(Ss, (lambda e, sl=slot: e.activation(
            xln[:, :], xch[:, sl * D:(sl + 1) * D],
            ACT.Identity, bias=lncols[:, 1:2], scale=lncols[:, 0:1])),
            waits=[P.w("lncols"), P.wo_("ln1tr")], inc=sS)
        P.mark("xln1n", sS)

        def srcT(j):
            return xln[:, j * CK:(j + 1) * CK]

        def dstT(g0, gn, ci=ci):
            r = ln1fm[:, :].rearrange("p (dc w) -> p dc w", dc=ND)
            return r[:, g0:g0 + gn, ci * CK:(ci + 1) * CK]

        transpose_group(srcT, ND, dstT,
                        [P.w("xln1n"), (sDC, cdone)],
                        extra_evac_waits=[P.wo_("ln1fm_free")])
        P.mark("ln1tr", sT)
        P.mark("ln1fm_c", sS)

    def fast_projMM(gc):
        sl2 = gc % 2
        lfm = [P.w("ln1fm_c")]
        bK, wK = banks.acquire()
        bV2, wV2 = banks.acquire()
        bB, wB = banks.acquire()
        for dc in range(ND):
            st, sp = dc == 0, dc == ND - 1
            lhs = ln1fm[:, dc * MAXW + sl2 * CK:dc * MAXW + (sl2 + 1) * CK]
            w0 = (list(wK) + list(wV2) + list(wB) + lfm +
                  [P.wo_("w_k"), P.wo_("w_v")]) if dc == 0 else []
            mm(pb[bK][:, :], lhs, wk_sb[:, dc * INNER:(dc + 1) * INNER], st, sp,
               waits=w0)
            mm(pb[bV2][:, :], lhs, wv_sb[:, dc * INNER:(dc + 1) * INNER], st, sp)
            mm(pb[bB][:, 0:1], lhs, wbar_sb[:, dc:dc + 1], st, sp,
               inc=(sT if sp else None))
        P.mark("rsf", sT)
        return bK, bV2, bB

    def fast_projrest(gc, bK, bV2, bB):
        sl2 = gc % 2
        bD, wD = banks.acquire()
        for dc in range(ND):
            mm(pb[bD][0:32, 0:CK], wsm_sb[:, dc * 32:(dc + 1) * 32],
               ln1fm[:, dc * MAXW + sl2 * CK:dc * MAXW + (sl2 + 1) * CK],
               dc == 0, dc == ND - 1,
               waits=(list(wD) if dc == 0 else []),
               inc=(sT if dc == ND - 1 else None))
        P.mark("drawm", sT)
        P.mark("ln1fm_free", sT)
        P.emit(Ss, (lambda e, b=bD, sl2=sl2: e.copy(
            draw_sb[:, sl2 * CK:(sl2 + 1) * CK], pb[b][0:32, 0:CK])),
            waits=[P.w("drawm"), P.wo_("draw_free")], consumes=[bD])
        P.mark("drawc", sS)
        P.emit(V, (lambda e, b=bK, sl2=sl2: e.tensor_copy(
            ktokr[:, sl2 * INNER:(sl2 + 1) * INNER], pb[b][:, :])),
            waits=[P.w("rsf"), P.wo_("ktok")], consumes=[bK])
        P.emit(V, (lambda e, b=bB: e.tensor_copy(bcol[:, 0:1], pb[b][:, 0:1])),
               consumes=[bB])
        P.emit(V, (lambda e, b=bV2, sl2=sl2: e.tensor_scalar(
            vbt[:, sl2 * INNER:(sl2 + 1) * INNER], pb[b][:, :], bcol[:, 0:1],
            None, OP.mult)), waits=[P.wo_("um")], consumes=[bV2], vhz=True)
        P.mark("kvb", sV)

    def fast_dt_a(gc):
        sl2 = gc % 2
        dt_path_a(draw_sb[:, sl2 * CK:(sl2 + 1) * CK])
        P.mark("draw_free", sT)

    def fast_b1(gc):
        ldc = ldt[:, :] if gc % 2 == 0 else vraw[:, 0:INNER]
        fast_eb[0] = eend_path(ldc, gc % 2)
        bD2, wD2 = banks.acquire()
        for j in range(4):
            mm(pb[bD2][:, j:j + 1], ldc[:, j * CK:(j + 1) * CK], ones_sb[:, :],
               True, True, waits=(list(wD2) if j == 0 else []),
               inc=(sT if j == 3 else None))
        P.mark("dcolm", sT)
        P.mark(f"ld_free{gc % 2}", sT)
        P.emit(Ss, (lambda e, b=bD2: e.activation(
            dcols[:, 0:4], pb[b][:, 0:4], ACT.Exp, scale=-1.0)),
            waits=[P.w("dcolm"), P.wo_("dcol_free")], consumes=[bD2])
        P.mark("dcol", sS)

    def fast_b2(gc):
        sl2 = gc % 2
        eb = fast_eb[0]
        P.emit(V, (lambda e, sl2=sl2, eb=eb: e.tensor_mul(
            kct[:, sl2 * INNER:(sl2 + 1) * INNER],
            ktokr[:, sl2 * INNER:(sl2 + 1) * INNER], eb)),
            waits=[P.w("eend"), P.wo_(f"um{sl2}")])
        P.mark(f"eend_free{sl2}", sV)
        P.mark("ktok", sV)
        bU, wU = banks.acquire()
        for h in range(H):
            j, r0 = h // 2, 64 * (h % 2)
            mm(pb[bU][r0:r0 + HD, j * HD:(j + 1) * HD],
               kct[:, sl2 * INNER + h * HD:sl2 * INNER + (h + 1) * HD],
               vbt[:, sl2 * INNER + h * HD:sl2 * INNER + (h + 1) * HD],
               True, True, tp=(0, r0),
               waits=(list(wU) + [P.w("ktok"), P.w("kvb")]) if h == 0 else [],
               inc=(sT if h == H - 1 else None))
        P.mark("um", sT)
        P.mark(f"um{sl2}", sT)
        state_chain(bU)
        P.mark("dcol_free", sV)

    def now(sem):
        return (sem, P.ctr[sem])

    recs = {}
    trecs = {}
    fast_eb = [None]

    def rec_of(g):
        return recs.get(g, {})

    def p1a(gc, ci):
        r = recs.setdefault(gc, {})
        bE, wE = banks.acquire()
        mm(pb[bE][:, :], draw_sb[:, ci * CK:(ci + 1) * CK], dtw_sb[:, :],
           True, True,
           waits=list(wE) + [trecs[gc // TCH]["drawc"], P.wo_("w_dt")], inc=sT)
        r["dt2"] = now(sT)
        P.emit(V, (lambda e, b=bE: e.tensor_add(spy[:, :], pb[b][:, :], dtbB[:, :])),
               waits=[r["dt2"], rec_of(gc - 1).get("sp_s"), P.wo_("w_bc"),
                      P.wo_("spy_free2")],
               consumes=[bE])
        spz = now(sV)
        P.emit(Ss, (lambda e: e.activation(spy[:, :], spy[:, :], ACT.Exp)),
               waits=[spz], inc=sS)
        P.emit(Ss, (lambda e: e.activation(spy[:, :], spy[:, :], ACT.Ln, bias=1.0)),
               inc=sS)
        r["sp_s"] = now(sS)
        P.emit(V, (lambda e: e.tensor_mul(ldt[:, :], spy[:, :], baseB[:, :])),
               waits=[r["sp_s"], rec_of(gc - 1).get("packm")])
        r["ld"] = now(sV)
        P.mark("ld2", sV)

    def p2(gc, ci):
        r = recs[gc]
        sl3 = gc % 2
        bKT, wKT = banks.acquire()
        for j in range(4):
            mm(pb[bKT][:, j * CK:(j + 1) * CK],
               kfm[:, j * MAXW + ci * CK:j * MAXW + (ci + 1) * CK],
               ident_sb[:, :], True, True,
               waits=(list(wKT) + [trecs[gc // TCH]["kfm"]]) if j == 0 else [],
               inc=(sT if j == 3 else None))
        r["ktr"] = now(sT)
        P.emit(V, (lambda e, b=bKT, sl3=sl3: e.tensor_copy(
            ktokr[:, sl3 * INNER:(sl3 + 1) * INNER], pb[b][:, :])),
            waits=[r["ktr"], rec_of(gc - 2).get("ktok"),
                   rec_of(gc - 2).get("kv")], consumes=[bKT])
        r["ktokc"] = now(sV)

    def p1b(gc, ci):
        r = recs[gc]
        sl2 = gc % 2
        ldc = ldt[:, :]
        bE2, wE2 = banks.acquire()
        mm(pb[bE2][:, :], msu_sb[:, :], ldc, True, True,
           waits=list(wE2) + [r["ld"]], inc=sT)
        eendm = now(sT)
        P.emit(Ss, (lambda e, b=bE2: e.activation(
            eend[:, :], pb[b][:, :], ACT.Exp, scale=-1.0)),
            waits=[eendm, rec_of(gc - 1).get("eendf")], consumes=[bE2])
        eendk = now(sS)
        evb = sl2 * 4 * 768
        expj = []
        for j in range(4):
            lds = ldc[:, j * CK:(j + 1) * CK]
            b1, w1 = banks.acquire()
            b2, w2 = banks.acquire()
            mm(pb[b1][:, 0:385], lds, p1a_sb[:, :], True, True,
               waits=list(w1) + list(w2) +
                     ([rec_of(gc - 2).get("ym")] if j == 0 else []))
            mm(pb[b2][:, 0:384], lds, p2n_sb[:, :], True, True, inc=sT)
            packm = now(sT)
            ew = ([rec_of(gc - 2).get("ym"), rec_of(gc - 2).get("chain")]
                  if j == 0 else [])
            P.emit(Ss, (lambda e, b=b1, j=j, sl2=sl2: e.activation(
                dcols[:, sl2 * 4 + j:sl2 * 4 + j + 1], pb[b][:, 384:385],
                ACT.Exp, scale=-1.0)),
                waits=[packm] + ew, inc=sS)
            P.emit(Ss, (lambda e, b=b1, j=j, evb=evb: e.activation(
                ev[:, evb + j * 768:evb + j * 768 + 384], pb[b][:, 0:384],
                ACT.Exp, scale=-1.0)), consumes=[b1])
            P.emit(Ss, (lambda e, b=b2, j=j, evb=evb: e.activation(
                ev[:, evb + j * 768 + 384:evb + (j + 1) * 768], pb[b][:, 0:384],
                ACT.Exp, scale=-1.0)), consumes=[b2])
            expj.append(now(sS))
        r["packm"] = now(sT)
        r["dcol"] = now(sS)
        P.emit(V, (lambda e, sl2=sl2: e.tensor_mul(
            kct[:, sl2 * INNER:(sl2 + 1) * INNER],
            ktokr[:, sl2 * INNER:(sl2 + 1) * INNER], eend[:, :])),
            waits=[eendk, r["ktokc"], rec_of(gc - 2).get("um")])
        r["eendf"] = now(sV)
        r["ktok"] = now(sV)
        P.emit(G, (lambda e, sl2=sl2: e.tensor_mul(
            kvprod[:, sl2 * INNER:(sl2 + 1) * INNER],
            ktokr[:, sl2 * INNER:(sl2 + 1) * INNER],
            vraw[:, 0:INNER])),
            waits=[r["vb"], r["ktokc"], rec_of(gc - 2).get("bny")])
        r["kv"] = now(sG)
        kf = kfm[:, ci * CK:ci * CK + CK]
        krep = bass.AP(kf.tensor, kf.offset,
                       [kf.ap[0], [MAXW, 4], [0, 6], kf.ap[-1]])
        P.emit(G, (lambda e, a=krep, evb=evb: e.tensor_mul(
            ev[:, evb:evb + 4 * 768], a, ev[:, evb:evb + 4 * 768])),
            waits=expj + [trecs[gc // TCH]["kfm"], rec_of(gc - 2).get("ym")])
        r["kvar"] = now(sG)

    def p3(ti, gc, ci):
        r = recs[gc]
        sl2 = gc % 2
        sl4 = gc % 4
        tslot = (ti % 2) * TCH + ci
        tr = trecs[ti]
        evb = sl2 * 4 * 768
        abanks = []
        for gq in range(2):
            bi, bw = banks.acquire()
            abanks.append(bi)
            for hh in range(4):
                h = gq * 4 + hh
                j, r0 = h // 2, 64 * (h % 2)
                col0 = hh * CK
                EV = ev[:, evb + j * 768:evb + (j + 1) * 768]
                w0 = (list(bw) + [r["kvar"]]) if hh == 0 else []
                mm(pb[bi][:, col0:col0 + CK], EV[r0:r0 + HD, 640:768],
                   EV[r0:r0 + HD, 512:640], True, True, tp=(r0, 0), waits=w0)
                mm(pb[bi][0:64, col0 + 64:col0 + 128], EV[r0:r0 + HD, 128:192],
                   EV[r0:r0 + HD, 192:256], True, True, tp=(r0, 0))
                mm(pb[bi][0:32, col0 + 32:col0 + 64], EV[r0:r0 + HD, 256:288],
                   EV[r0:r0 + HD, 288:320], True, True, tp=(r0, 0))
                mm(pb[bi][64:96, col0 + 96:col0 + 128], EV[r0:r0 + HD, 320:352],
                   EV[r0:r0 + HD, 352:384], True, True, tp=(r0, 64))
                mm(pb[bi][0:16, col0 + 16:col0 + 32], EV[r0:r0 + HD, 384:400],
                   EV[r0:r0 + HD, 400:416], True, True, tp=(r0, 0))
                mm(pb[bi][32:48, col0 + 48:col0 + 64], EV[r0:r0 + HD, 416:432],
                   EV[r0:r0 + HD, 432:448], True, True, tp=(r0, 32))
                mm(pb[bi][64:80, col0 + 80:col0 + 96], EV[r0:r0 + HD, 448:464],
                   EV[r0:r0 + HD, 464:480], True, True, tp=(r0, 64))
                mm(pb[bi][96:112, col0 + 112:col0 + 128], EV[r0:r0 + HD, 480:496],
                   EV[r0:r0 + HD, 496:512], True, True, tp=(r0, 96),
                   inc=(sT if hh == 3 else None))
        r["am"] = now(sT)
        for gq, bi in enumerate(abanks):
            P.emit(V, (lambda e, b=bi, gq=gq: e.copy_predicated(
                a_sb[:, gq * 4 * CK:(gq * 4 + 4) * CK],
                mask_sb[:, :], pb[b][:, :])),
                waits=[r["am"]] + ([rec_of(gc - 1).get("ym")] if gq == 0 else []),
                consumes=[bi], vhz=True)
        asel = now(sV)
        chain_prev = rec_of(gc - 1).get("chain") or P.w("chain")
        bY, wY = banks.acquire()
        for h in range(H):
            j, r0 = h // 2, 64 * (h % 2)
            kg = ev[:, evb + j * 768:evb + j * 768 + CK]
            w0 = (list(wY) + [chain_prev, asel, r["vb"]]) if h == 0 else []
            mm(pb[bY][:, h * HD:(h + 1) * HD], kg[r0:r0 + HD, :],
               Sbf[r0:r0 + HD, j * HD:(j + 1) * HD], True, False, tp=(r0, 0),
               waits=w0)
            mm(pb[bY][:, h * HD:(h + 1) * HD],
               a_sb[:, h * CK:(h + 1) * CK],
               vbt[:, sl4 * INNER + h * HD:sl4 * INNER + (h + 1) * HD],
               False, True, tp=(0, 0),
               inc=(sT if h == H - 1 else None))
        r["ym"] = now(sT)
        bU, wU = banks.acquire()
        for h in range(H):
            j, r0 = h // 2, 64 * (h % 2)
            mm(pb[bU][r0:r0 + HD, j * HD:(j + 1) * HD],
               kct[:, sl2 * INNER + h * HD:sl2 * INNER + (h + 1) * HD],
               vbt[:, sl4 * INNER + h * HD:sl4 * INNER + (h + 1) * HD],
               True, True, tp=(0, r0),
               waits=(list(wU) + [r["ktok"], r["vb"]]) if h == 0 else [],
               inc=(sT if h == H - 1 else None))
        r["um"] = now(sT)
        P.emit(V, (lambda e, sl2=sl2: e.tensor_mul(
            S_sb[:, :], S_sb[:, :], dcol_rep(sl2 * 4))),
            waits=[r["um"], r["dcol"], r["ym"]])
        P.emit(V, (lambda e, b=bU: e.tensor_add(S_sb[:, :], S_sb[:, :],
                                                pb[b][:, 0:4 * HD])),
               consumes=[bU])
        P.emit(V, lambda e: e.tensor_copy(Sbf[:, :], S_sb[:, :]))
        r["chain"] = now(sV)
        P.mark("chain", sV)
        P.emit(V, (lambda e, b=bY, sl2=sl2: e.tensor_add(
            bny[:, :], kvprod[:, sl2 * INNER:(sl2 + 1) * INNER], pb[b][:, :])),
            waits=[r["ym"], r["kv"]], consumes=[bY])
        r["bny"] = now(sV)
        P.emit(G, (lambda e, tslot=tslot, sl2=sl2: e.tensor_mul(
            ys_sb[:, sl2 * INNER:(sl2 + 1) * INNER],
            gp_sb[:, tslot * INNER:(tslot + 1) * INNER], bny[:, :])),
            waits=[tr["thrg"], r["bny"], rec_of(gc - 2).get("ystr")])
        r["ysb2"] = now(sG)
        r["ys"] = now(sG)

    def p4a(ti, gc, ci):
        r = recs[gc]
        slot = gc % 5
        sl2 = gc % 2
        bYS, wYS = banks.acquire()
        for icb in range(NI):
            mm(pb[bYS][:, icb * CK:(icb + 1) * CK],
               ys_sb[:, sl2 * INNER + icb * CK:sl2 * INNER + (icb + 1) * CK],
               ident_sb[:, :], True, True,
               waits=(list(wYS) + [r["ys"]]) if icb == 0 else [],
               inc=(sT if icb == NI - 1 else None))
        r["ystr"] = now(sT)
        P.emit(Ss, (lambda e, b=bYS: e.copy(ysfm[:, :], pb[b][:, :])),
               waits=[r["ystr"], rec_of(gc - 1).get("wo_done")], consumes=[bYS])
        ysfmr = now(sS)
        for half in range(2):
            bW, wW = banks.acquire()
            for icb in range(NI):
                mm(pb[bW][:, 0:INNER],
                   ysfm[:, icb * CK:(icb + 1) * CK],
                   wo_sb[:, icb * D + half * INNER:icb * D + (half + 1) * INNER],
                   icb == 0, icb == NI - 1,
                   waits=(list(wW) + ([ysfmr, P.wo_("w_o")]
                                      if half == 0 else [])) if icb == 0 else [],
                   inc=(sT if icb == NI - 1 else None))
            wo_m = now(sT)
            xsrc = xch[:, slot * D + half * INNER:slot * D + (half + 1) * INNER]
            dst = x2[:, ci * D + half * INNER:ci * D + (half + 1) * INNER]
            wx = [wo_m, rec_of(gc - 3).get("xln2n")]
            if P.ctr[sDO] > 0:
                wx.append((sDO, P.ctr[sDO]))
            P.emit(V, (lambda e, d=dst, a=xsrc, b=bW: e.tensor_add(
                d, a, pb[b][:, 0:INNER])), waits=wx, consumes=[bW])
        r["wo_done"] = now(sT)
        r["x2"] = now(sV)

    def p4b_stats(ti, gc, ci):
        r = recs[gc]
        lnc2 = ln_stats(x2[:, ci * D:(ci + 1) * D], 4, 5, r["x2"], so=3)
        P.emit(Ss, (lambda e, ci=ci: e.activation(
            xln2[:, :], x2[:, ci * D:(ci + 1) * D],
            ACT.Identity, bias=lncols[:, 5:6], scale=lncols[:, 4:5])),
            waits=[lnc2, P.wo_("ln2tr")], inc=sS)
        r["xln2n"] = now(sS)
        P.mark("xln2n", sS)

    def p4b(ti, gc, ci):
        r = recs[gc]

        def srcT2(jj):
            return xln2[:, jj * CK:(jj + 1) * CK]

        def dstT2(g0, gn, ci=ci):
            rr = x2fm[:, :].rearrange("p (dc w) -> p dc w", dc=ND)
            return rr[:, g0:g0 + gn, 1 + ci * CK:1 + (ci + 1) * CK]

        transpose_group(srcT2, ND, dstT2, [r["xln2n"]],
                        extra_evac_waits=[P.wo_("mix")])
        P.mark("ln2tr", sT)
        r["x2fm"] = now(sS)

    def prologue(ti, gc0, ncint):
        W = ncint * CK
        tr = trecs.setdefault(ti, {})
        last_prev = rec_of(gc0 - 1)
        for ci in range(ncint):
            gc = gc0 + ci
            slot = gc % 5
            ln1_chunk("xm", gc, slot, ci)
        tr["ln1"] = now(sS)
        lfm = [tr["ln1"]]
        for oc in range(NI):
            bi, bw = banks.acquire()
            for dc in range(ND):
                mm(pb[bi][:, 0:W],
                   wk_sb[:, dc * INNER + oc * CK:dc * INNER + (oc + 1) * CK],
                   ln1fm[:, dc * MAXW:dc * MAXW + W], dc == 0, dc == ND - 1,
                   waits=(list(bw) + (lfm +
                          [last_prev.get("ktr"), last_prev.get("kvar")]
                          if oc == 0 else [])) if dc == 0 else [],
                   inc=(sT if dc == ND - 1 else None))
            kfm_m = now(sT)
            P.emit(Ss, (lambda e, oc=oc, W=W, b=bi: e.copy(
                kfm[:, oc * MAXW:oc * MAXW + W], pb[b][:, 0:W])),
                waits=[kfm_m], consumes=[bi])
        tr["kfm"] = now(sS)
        bD, wD = banks.acquire()
        for dc in range(ND):
            mm(pb[bD][0:32, 0:W], wsm_sb[:, dc * 32:(dc + 1) * 32],
               ln1fm[:, dc * MAXW:dc * MAXW + W], dc == 0, dc == ND - 1,
               waits=(list(wD) + ([last_prev.get("dt2")] if dc == 0 else []))
                     if dc == 0 else [],
               inc=(sT if dc == ND - 1 else None))
        drawm = now(sT)
        P.emit(Ss, (lambda e, W=W, b=bD: e.copy(draw_sb[:, 0:W], pb[b][0:32, 0:W])),
               waits=[drawm, last_prev.get("dt2")], consumes=[bD])
        tr["drawc"] = now(sS)

    def rgvb(ti, gc0, ncint):
        tr = trecs[ti]
        for c in range(ncint):
            tslot = (ti % 2) * TCH + c
            bR, wR = banks.acquire()
            bG, wG = banks.acquire()
            for dc in range(ND):
                st, sp = dc == 0, dc == ND - 1
                lhs = ln1fm[:, dc * MAXW + c * CK:dc * MAXW + (c + 1) * CK]
                w0 = (list(wR) + list(wG) +
                      ([P.wo_("w_r"), P.wo_("w_g"), tr["ln1"]]
                       if c == 0 else [])) if dc == 0 else []
                mm(pb[bR][:, :], lhs, wr_sb[:, dc * INNER:(dc + 1) * INNER],
                   st, sp, waits=w0)
                mm(pb[bG][:, :], lhs, wg_sb[:, dc * INNER:(dc + 1) * INNER],
                   st, sp, inc=(sT if sp else None))
            rgm = now(sT)
            P.emit(Ss, (lambda e, b=bR: e.activation(
                rgtmp[:, 0:INNER], pb[b][:, :], ACT.Sigmoid)),
                   waits=[rgm, P.wo_("gpv")], consumes=[bR])
            P.emit(Ss, (lambda e, b=bG: e.activation(
                rgtmp[:, INNER:2 * INNER], pb[b][:, :], ACT.Sigmoid)),
                consumes=[bG])
            gprev = recs.get(gc0 - 2 * TCH + c, {}).get("ysb2")
            P.emit(V, (lambda e, tslot=tslot: e.tensor_mul(
                gp_sb[:, tslot * INNER:(tslot + 1) * INNER], rgtmp[:, 0:INNER],
                rgtmp[:, INNER:2 * INNER])),
                waits=[now(sS), gprev])
            P.mark("gpv", sV)
        tr["thrg"] = now(sV)

    def pvb(gc, ci):
        r = recs[gc]
        sl2 = gc % 2
        sl4 = gc % 4
        bV2, wV2 = banks.acquire()
        bB, wB = banks.acquire()
        for dc in range(ND):
            st, sp = dc == 0, dc == ND - 1
            lhs = ln1fm[:, dc * MAXW + ci * CK:dc * MAXW + (ci + 1) * CK]
            w0 = (list(wV2) + list(wB)) if dc == 0 else []
            mm(pb[bV2][:, :], lhs, wv_sb[:, dc * INNER:(dc + 1) * INNER],
               st, sp, waits=w0)
            mm(pb[bB][:, 0:1], lhs, wbar_sb[:, dc:dc + 1], st, sp,
               inc=(sT if sp else None))
        rsv = now(sT)
        P.emit(V, (lambda e, b=bB: e.tensor_copy(bcol[:, 0:1], pb[b][:, 0:1])),
               waits=[rsv], consumes=[bB])
        P.emit(V, (lambda e, b=bV2: e.tensor_mul(
            vraw[:, 0:INNER], pb[b][:, :], tfB[:, :])))
        P.emit(V, (lambda e, b=bV2, sl4=sl4: e.tensor_scalar(
            vbt[:, sl4 * INNER:(sl4 + 1) * INNER], pb[b][:, :],
            bcol[:, 0:1], None, OP.mult)),
            waits=[rec_of(gc - 4).get("um"), rec_of(gc - 4).get("ym")],
            consumes=[bV2], vhz=True)
        r["vb"] = now(sV)

    def ffn_mixes(ti, gc0, ncint):
        W = ncint * CK
        r_last = recs[gc0 + ncint - 1]
        for dc in range(ND):
            xcur = x2fm[:, dc * (MAXW + 1) + 1:dc * (MAXW + 1) + 1 + W]
            xprv = x2fm[:, dc * (MAXW + 1):dc * (MAXW + 1) + W]
            P.emit(V, (lambda e, a=xcur, b=xprv, W=W: e.tensor_sub(
                mixd[:, 0:W], a, b)),
                waits=([r_last["x2fm"], P.wo_("rffn")]
                       if dc == 0 else []))
            P.emit(V, (lambda e, b=xprv, dc=dc, W=W: e.scalar_tensor_tensor(
                xr_sb[:, dc * MAXW:dc * MAXW + W], mixd[:, 0:W],
                mur_sb[:, dc:dc + 1], b, OP.mult, OP.add)))
            P.emit(V, (lambda e, b=xprv, dc=dc, W=W: e.scalar_tensor_tensor(
                xk_sb[:, dc * MAXW:dc * MAXW + W], mixd[:, 0:W],
                muk_sb[:, dc:dc + 1], b, OP.mult, OP.add)))
        P.emit(V, (lambda e, W=W: e.tensor_copy(
            x2fm[:, :].rearrange("p (dc w) -> p dc w", dc=ND)[:, :, 0:1],
            x2fm[:, :].rearrange("p (dc w) -> p dc w", dc=ND)[:, :, W:W + 1])))
        P.mark("mix", sV)

    def ffn_rest(ti, gc0, ncint):
        W = ncint * CK
        mixw = P.w("mix")
        for oc in range(ND):
            bi, bw = banks.acquire()
            for cc in range(ND):
                mm(pb[bi][:, 0:W],
                   fwkT[:, (cc * ND + oc) * CK:(cc * ND + oc + 1) * CK],
                   xk_sb[:, cc * MAXW:cc * MAXW + W], cc == 0, cc == ND - 1,
                   waits=(list(bw) + ([mixw, P.wo_("w_fk")]
                                      if oc == 0 else [])) if cc == 0 else [],
                   inc=(sT if cc == ND - 1 else None))
            kffn = now(sT)
            P.emit(V, (lambda e, oc=oc, W=W, b=bi: e.tensor_scalar(
                kact[:, oc * MAXW:oc * MAXW + W], pb[b][:, 0:W], 0.0, None,
                OP.max)),
                waits=[kffn] + ([P.wo_("vffn")] if oc == 0 else []),
                consumes=[bi])
            P.emit(Ss, (lambda e, oc=oc, W=W: e.activation(
                kact[:, oc * MAXW:oc * MAXW + W],
                kact[:, oc * MAXW:oc * MAXW + W], ACT.Square)),
                waits=[now(sV)])
        P.mark("kact", sS)
        for ci in range(ncint):
            for half in range(2):
                bR, wR = banks.acquire()
                for cc in range(ND):
                    mm(pb[bR][:, 0:INNER],
                       xr_sb[:, cc * MAXW + ci * CK:cc * MAXW + (ci + 1) * CK],
                       fwrT[:, (cc * ND + half * 4) * CK:(cc * ND + half * 4 + 4) * CK],
                       cc == 0, cc == ND - 1,
                       waits=(list(wR) + ([P.wo_("w_fr")]
                                          if (ci == 0 and half == 0) else []))
                             if cc == 0 else [],
                       inc=(sT if cc == ND - 1 else None))
                P.mark("rffn", sT)
                P.emit(Ss, (lambda e, b=bR: e.activation(
                    spy[:, 0:INNER], pb[b][:, 0:INNER], ACT.Sigmoid)),
                    waits=[P.w("rffn"), P.wo_("spy_free2"), P.wo_("ld2")],
                    consumes=[bR])
                P.mark("thf", sS)
                bV3, wV3 = banks.acquire()
                for cc in range(ND):
                    mm(pb[bV3][:, 0:INNER],
                       kact[:, cc * MAXW + ci * CK:cc * MAXW + (ci + 1) * CK],
                       fwvT[:, (cc * ND + half * 4) * CK:(cc * ND + half * 4 + 4) * CK],
                       cc == 0, cc == ND - 1,
                       waits=(list(wV3) + ([P.w("kact"), P.wo_("w_fv")]
                                           if (ci == 0 and half == 0) else []))
                             if cc == 0 else [],
                       inc=(sT if cc == ND - 1 else None))
                P.mark("vffn", sT)
                P.emit(V, (lambda e, b=bV3: e.tensor_mul(
                    spy[:, 0:INNER], spy[:, 0:INNER], pb[b][:, 0:INNER])),
                    waits=[P.w("vffn"), P.w("thf")],
                    consumes=[bV3])
                P.mark("ffnt", sV)
                dst = x2[:, ci * D + half * INNER:ci * D + (half + 1) * INNER]
                P.emit(V, (lambda e, d=dst: e.tensor_add(d, d, spy[:, 0:INNER])))
                P.mark("spy_free2", sV)
        x3 = now(sV)
        for ci in range(ncint):
            gc = gc0 + ci
            if gc == 0:
                continue
            src2 = x2[:, ci * D:(ci + 1) * D]
            dst = out[(gc - 1) * CK:gc * CK, :]
            P.emit(SY, (lambda e, d=dst, s=src2: e.dma_start(d, s)),
                   waits=[x3], inc=sDO, inc_n=16)

    FST = 2
    dma_x(xf, "xf", FST, FST % 3, waits=[(sDC, cdone)])
    dma_x(xf, "xf", FST + 1, (FST + 1) % 3)
    fast_ln1a(FST)
    fast_ln1b(FST)
    fast_ln1a(FST + 1)
    fast_ln1b(FST + 1)
    _fb = fast_projMM(FST)
    fast_projrest(FST, *_fb)
    _fb = fast_projMM(FST + 1)
    fast_projrest(FST + 1, *_fb)
    fast_dt_a(FST)
    dt_path_b(FST)
    for g in range(FST, NCH):
        fast_b1(g)
        fast_b2(g)
        if g + 1 < NCH:
            fast_dt_a(g + 1)
        if g + 2 < NCH:
            fast_ln1a(g + 2)
        if g + 1 < NCH:
            dt_path_b(g + 1)
        if g + 2 < NCH:
            fast_ln1b(g + 2)
            _fb = fast_projMM(g + 2)
            fast_projrest(g + 2, *_fb)
        pump_loader(2)
    pump_loader(1000)

    for gseed in range(3):
        dma_x(xm, "xm", gseed, gseed % 5, waits=[P.wo_("xln1n")])

    snap = [(s, P.ctr[s]) for s in (sT, sV, sS)]
    P.barrier((T, V, Ss, G), snap)

    for g in range(NCH + 4):
        if 0 <= g - 2 < NCH:
            go = g - 2
            p3(go // TCH, go, go % TCH)
        if 0 <= g - 4 < NCH:
            go = g - 4
            p4b(go // TCH, go, go % TCH)
            if go % TCH == TCH - 1:
                to = go // TCH
                ffn_mixes(to, to * TCH, TCH)
                ffn_rest(to, to * TCH, TCH)
        if 0 <= g - 3 < NCH:
            go = g - 3
            p4a(go // TCH, go, go % TCH)
            p4b_stats(go // TCH, go, go % TCH)
        if 1 <= g < NCH and g + 2 < NCH:
            dma_x(xm, "xm", g + 2, (g + 2) % 5,
                  waits=[rec_of(g - 3).get("x2"), P.wo_("xln1n")])
        if g < NCH:
            ti, ci = g // TCH, g % TCH
            if ci == 0:
                prologue(ti, ti * TCH, TCH)
            p1a(g, ci)
            pvb(g, ci)
            p2(g, ci)
            p1b(g, g % TCH)
            if ci == 0:
                rgvb(ti, ti * TCH, TCH)

    P.emit(G, lambda e: e.memset(bcol[:, 1:2], 0.0), waits=[(sDO, P.ctr[sDO])])

    P.finalize()
    stack.close()
    return nc


def kernel(**inputs):
    x = np.asarray(inputs["L0_attn_x"])
    consts = host_constants()
    nc = build_program()

    base = np.asarray(inputs["L0_attn_base_decay"]).reshape(1, INNER)
    dtb = np.asarray(inputs["L0_attn_dt_b"]).reshape(1, INNER)
    tf = np.asarray(inputs["L0_attn_time_first"]).reshape(1, INNER)
    mur_h = np.ascontiguousarray(
        np.asarray(inputs["L0_ffn_mu_r"]).reshape(ND, CK).T, dtype=np.float32)
    muk_h = np.ascontiguousarray(
        np.asarray(inputs["L0_ffn_mu_k"]).reshape(ND, CK).T, dtype=np.float32)

    common = dict(
        wsel=np.ascontiguousarray(inputs["L0_attn_W_sel"], dtype=np.float32),
        dtw=np.ascontiguousarray(inputs["L0_attn_dt_W"], dtype=np.float32),
        wk=np.ascontiguousarray(inputs["L0_attn_Wk"], dtype=np.float32),
        wv=np.ascontiguousarray(inputs["L0_attn_Wv"], dtype=np.float32),
        wr=np.ascontiguousarray(inputs["L0_attn_Wr"], dtype=np.float32),
        wg=np.ascontiguousarray(inputs["L0_attn_Wg"], dtype=np.float32),
        wo=np.ascontiguousarray(inputs["L0_attn_Wo"], dtype=np.float32),
        fwr=np.ascontiguousarray(inputs["L0_ffn_W_r"], dtype=np.float32),
        fwk=np.ascontiguousarray(inputs["L0_ffn_W_k"], dtype=np.float32),
        fwv=np.ascontiguousarray(inputs["L0_ffn_W_v"], dtype=np.float32),
        dtb_b=np.ascontiguousarray(np.broadcast_to(dtb, (CK, INNER)),
                                   dtype=np.float32),
        base_b=np.ascontiguousarray(np.broadcast_to(base, (CK, INNER)),
                                    dtype=np.float32),
        tf_b=np.ascontiguousarray(np.broadcast_to(tf, (CK, INNER)),
                                  dtype=np.float32),
        mur=mur_h,
        muk=muk_h,
        **consts,
    )

    in_maps = []
    for c in range(8):
        b, h = c // 2, c % 2
        T0 = 4096 * h
        xmain = np.zeros((ROWS, D), np.float32)
        if h == 0:
            xmain[CK:] = x[b, 0:4096]
        else:
            xmain[:] = x[b, T0 - CK:T0 + 4096]
        xfast = np.zeros((ROWS, D), np.float32)
        if h == 1:
            xfast[2 * CK:] = x[b, 0:4096 - CK]
        m = dict(common)
        m["xm"] = xmain
        m["xf"] = xfast
        in_maps.append(m)

    res = run_bass_kernel_spmd(nc, in_maps, core_ids=list(range(8)))
    y = np.empty((4, 8192, D), np.float32)
    for c in range(8):
        b, h = c // 2, c % 2
        y[b, 4096 * h:4096 * (h + 1)] = np.asarray(res.results[c]["out"])
    return y
